# version 27
# baseline (speedup 1.0000x reference)
import numpy as np
import concourse.bass as bass
import concourse.mybir as mybir
from concourse.bass_utils import run_bass_kernel_spmd
from contextlib import ExitStack

F32 = mybir.dt.float32
BF16 = mybir.dt.bfloat16
AF = mybir.ActivationFunctionType
ALU = mybir.AluOpType

D = 1024
NCH = 8
H = 8
HD = 128
CH = 128
TILE = 256
MSH = 44.0
EPS_LN = 1e-5
EPS_RMS = 1e-6
INV_D = 1.0 / D
E_MSH = float(np.exp(-MSH))


class Bld:
    ENGS = ("sync", "vector", "scalar", "gpsimd", "tensor")

    def __init__(self, nc, ctx, sem_names):
        self.nc = nc
        self.block = ctx.enter_context(nc.Block())
        self.sem = {}
        self.cnt = {}
        for name in sem_names:
            self.sem[name] = ctx.enter_context(nc.semaphore(name))
            self.cnt[name] = 0
        self.q = {e: [] for e in self.ENGS}

    def op(self, eng, fn, wait=(), inc=(), nop=False):
        wmax = {}
        for s, v in wait:
            if v > 0:
                wmax[s] = max(v, wmax.get(s, 0))
        waits = tuple((self.sem[s], v) for s, v in wmax.items())
        ev = None
        incs = []
        for s, k in inc:
            self.cnt[s] += k
            incs.append((self.sem[s], k))
            ev = (s, self.cnt[s])

        def closure(e, fn=fn, waits=waits, incs=tuple(incs), nop=nop):
            for sem, v in waits:
                e.wait_ge(sem, v)
            if nop:
                e.engine_nop()
            ins = fn(e)
            for sem, k in incs:
                ins.then_inc(sem, k)

        self.q[eng].append(closure)
        return ev

    def wait(self, eng, *evs):
        wmax = {}
        for s, v in evs:
            if v > 0:
                wmax[s] = max(v, wmax.get(s, 0))
        waits = tuple((self.sem[s], v) for s, v in wmax.items())

        def closure(e, waits=waits):
            for sem, v in waits:
                e.wait_ge(sem, v)

        self.q[eng].append(closure)

    def flush(self):
        for e in self.ENGS:
            ops = self.q[e]
            if not ops:
                continue
            self.q[e] = []

            def body(engine, ops=ops):
                for f in ops:
                    f(engine)

            getattr(self.block, e)(body)

    def ev(self, s):
        return (s, self.cnt[s])


def build_program(tok, warm, debug=False):
    assert tok % TILE == 0 and warm % TILE == 0
    nt = tok // TILE
    out_tok = tok - warm

    nc = bass.Bass("TRN2", target_bir_lowering=False, dynamic_dma_scratch_size=2048)

    xT = nc.dram_tensor("xT", [128, tok // TILE, NCH, TILE], F32,
                        kind="ExternalInput").ap()
    w_in = {nm: nc.dram_tensor(nm, [D, D], F32, kind="ExternalInput").ap()
            for nm in ("wqT", "wkT", "wvT", "woT", "wrT", "wk2T", "wv2T")}
    nwv = nc.dram_tensor("nwv", [128, NCH], F32, kind="ExternalInput").ap()
    murv = nc.dram_tensor("murv", [128, NCH], F32, kind="ExternalInput").ap()
    mukv = nc.dram_tensor("mukv", [128, NCH], F32, kind="ExternalInput").ap()
    identv = nc.dram_tensor("identv", [128, 128], F32, kind="ExternalInput").ap()
    mask4v = nc.dram_tensor("mask4v", [128, 512], BF16, kind="ExternalInput").ap()
    onescv = nc.dram_tensor("onescv", [128, 1], F32, kind="ExternalInput").ap()
    onesrv = nc.dram_tensor("onesrv", [1, 128], F32, kind="ExternalInput").ap()

    outT = nc.dram_tensor("outT", [D, out_tok], F32, kind="ExternalOutput").ap()
    okind = dict(kind="ExternalOutput") if debug else {}
    o_scr = nc.dram_tensor("o_scr", [128, tok // TILE, NCH, TILE], F32, **okind).ap()
    rms_scr = nc.dram_tensor("rms_scr", [1, tok], F32, **okind).ap()
    x2_scr = nc.dram_tensor("x2_scr", [D, tok], F32, kind="ExternalOutput").ap() \
        if debug else None

    outTc = outT.rearrange("(c p) n -> p c n", p=128)

    ctx = ExitStack()
    bld = Bld(nc, ctx, ["LX", "LO", "LW", "LR", "SO0", "SO1", "SR", "SB", "SX",
                        "SW", "LB", "TT", "V", "S", "G", "T"])
    op, ev, C = bld.op, bld.ev, bld.cnt

    mask4 = ctx.enter_context(nc.sbuf_tensor("mask4", [128, 512], BF16))
    onesc = ctx.enter_context(nc.sbuf_tensor("onesc", [128, 1], F32))
    onescb = ctx.enter_context(nc.sbuf_tensor("onescb", [128, 1], BF16))
    onesr = ctx.enter_context(nc.sbuf_tensor("onesr", [1, 128], F32))
    nw = ctx.enter_context(nc.sbuf_tensor("nw", [128, NCH], F32))
    mur = ctx.enter_context(nc.sbuf_tensor("mur", [128, NCH], F32))
    muk = ctx.enter_context(nc.sbuf_tensor("muk", [128, NCH], F32))
    b_p44 = ctx.enter_context(nc.sbuf_tensor("b_p44", [128, 1], F32))
    b_m44 = ctx.enter_context(nc.sbuf_tensor("b_m44", [128, 1], F32))
    eps5 = ctx.enter_context(nc.sbuf_tensor("eps5", [1, 1], F32))
    eps6 = ctx.enter_context(nc.sbuf_tensor("eps6", [1, 1], F32))
    r_mean = ctx.enter_context(nc.sbuf_tensor("r_mean", [1, TILE], F32))
    r_msq = ctx.enter_context(nc.sbuf_tensor("r_msq", [1, TILE], F32))
    r_var = ctx.enter_context(nc.sbuf_tensor("r_var", [1, TILE], F32))
    r_a = ctx.enter_context(nc.sbuf_tensor("r_a", [1, TILE], F32))
    r_b = ctx.enter_context(nc.sbuf_tensor("r_b", [1, TILE], F32))
    r_rms = ctx.enter_context(nc.sbuf_tensor("r_rms", [1, TILE], F32))
    r_rr = r_rms
    r_t0 = r_msq
    wstage = ctx.enter_context(nc.sbuf_tensor("wstage", [128, 1, D], F32))

    p_pj = [ctx.enter_context(nc.psum_tensor(f"p_pj{i}", [128, 512], F32)) for i in range(2)]
    p_st = ctx.enter_context(nc.psum_tensor("p_st", [128, 512], F32))
    p_bc = ctx.enter_context(nc.psum_tensor("p_bc", [128, 512], F32))
    p_A = ctx.enter_context(nc.psum_tensor("p_A", [128, 512], F32))
    p_O = ctx.enter_context(nc.psum_tensor("p_O", [128, 512], F32))
    p_O2 = ctx.enter_context(nc.psum_tensor("p_O2", [128, 512], F32))
    p_A2 = ctx.enter_context(nc.psum_tensor("p_A2", [128, 512], F32))

    for dst, src in ((mask4, mask4v), (onesc, onescv), (onesr, onesrv),
                     (nw, nwv), (mur, murv), (muk, mukv)):
        op("sync", lambda e, dst=dst, src=src: e.dma_start(out=dst[:], in_=src),
           inc=[("LW", 16)])
    ev_cdma = ev("LW")
    op("vector", lambda e: e.memset(onescb[:], 1.0), inc=[("V", 1)], nop=True)
    op("vector", lambda e: e.memset(b_p44[:], MSH), inc=[("V", 1)], nop=True)
    op("vector", lambda e: e.memset(b_m44[:], -MSH), inc=[("V", 1)], nop=True)
    op("vector", lambda e: e.memset(eps5[:], EPS_LN), inc=[("V", 1)], nop=True)
    op("vector", lambda e: e.memset(eps6[:], EPS_RMS), inc=[("V", 1)], nop=True)

    def load_split_weight(src_ap, dst_h, dst_l, scale_nw=False, first_wait=(), eng="vector"):
        E = "V" if eng == "vector" else "G"
        srcc = src_ap.rearrange("(c p) n -> p c n", p=128)
        last = None
        for ci in range(NCH):
            wv = [(E, C[E])] + (list(first_wait) if ci == 0 else [])
            op("sync", lambda e, ci=ci, srcc=srcc: e.dma_start(
                out=wstage[:], in_=srcc[:, ci:ci + 1, :]), wait=wv, inc=[("LW", 16)])
            wl = [("LW", C["LW"]), ev_cdma]
            if scale_nw:
                op(eng, lambda e, ci=ci: e.tensor_scalar_mul(
                    wstage[:, 0, :], wstage[:, 0, :], nw[:, ci:ci + 1]),
                   wait=wl + [(E, C[E])], inc=[(E, 1)])
                wl = [(E, C[E])]
            ev_h = op("scalar", lambda e, dst_h=dst_h, ci=ci: e.activation(
                out=dst_h[:, ci, :], in_=wstage[:, 0, :], func=AF.Copy),
                wait=wl + [("S", C["S"])], inc=[("S", 1)])
            op(eng, lambda e, dst_h=dst_h, dst_l=dst_l, ci=ci: e.tensor_tensor(
                dst_l[:, ci:ci + 1, :], wstage[:], dst_h[:, ci:ci + 1, :], ALU.subtract),
               wait=[ev_h, (E, C[E])], inc=[(E, 1)])
            last = ev(E)
        return last

    state = {
        "bc_read": ("V", 0),
        "bc_ev": ("T", 0),
        "pj_free": [("V", 0), ("V", 0)],
        "stats_free": ("V", 0),
    }

    def row_stats(eps_tile):
        op("vector", lambda e: e.tensor_scalar_mul(r_mean[:], p_st[0:1, 0:TILE], INV_D),
           wait=[("T", C["T"]), ("V", C["V"])], inc=[("V", 1)])
        op("vector", lambda e: e.tensor_tensor(r_msq[:], r_mean[:], r_mean[:], ALU.mult),
           wait=[("V", C["V"])], inc=[("V", 1)])
        op("vector", lambda e: e.scalar_tensor_tensor(
            r_var[:], p_st[32:33, 0:TILE], INV_D, r_msq[:], ALU.mult, ALU.subtract),
           wait=[("V", C["V"])], inc=[("V", 1)])
        op("scalar", lambda e: e.activation(out=r_t0[:], in_=r_var[:], func=AF.Sqrt,
                                            bias=eps_tile[:, 0:1]),
           wait=[("V", C["V"]), ("S", C["S"])], inc=[("S", 1)])
        op("vector", lambda e: e.reciprocal(r_a[:], r_t0[:]),
           wait=[("S", C["S"])], inc=[("V", 1)], nop=True)
        op("vector", lambda e: e.scalar_tensor_tensor(
            r_b[:], r_mean[:], -1.0, r_a[:], ALU.mult, ALU.mult),
           wait=[("V", C["V"])], inc=[("V", 1)])
        return ev("V")

    def bcast_rows():
        wv = [("V", C["V"]), state["bc_read"]]
        op("tensor", lambda e: e.matmul(p_bc[:, 0:TILE], onesr[:], r_a[:],
                                        start=True, stop=False), wait=wv)
        op("tensor", lambda e: e.matmul(p_bc[:, 256:256 + TILE], onesr[:], r_b[:],
                                        start=False, stop=True), inc=[("T", 1)])
        return ev("T")

    def bcA():
        return p_bc[:, 0:TILE].unsqueeze(1).broadcast_to([128, NCH, TILE])

    def bcB():
        return p_bc[:, 256:256 + TILE].unsqueeze(1).broadcast_to([128, NCH, TILE])

    def stats_reduce(x_ap, sq_ap, first_wait=()):
        wv = [state["stats_free"]] + list(first_wait)
        for ci in range(NCH):
            op("tensor", lambda e, ci=ci: e.matmul(
                p_st[0:1, 0:TILE], onesc[:], x_ap[:, ci, :],
                start=(ci == 0), stop=(ci == NCH - 1)),
               wait=wv if ci == 0 else ())
            op("tensor", lambda e, ci=ci: e.matmul(
                p_st[32:33, 0:TILE], onescb[:], sq_ap[:, ci, :],
                start=(ci == 0), stop=(ci == NCH - 1)),
               inc=[("T", 1)] if ci == NCH - 1 else ())
        return ev("T")

    def proj_feature(wh, wl, xh, xl, dj, extra_wait=()):
        bank = p_pj[dj % 2]
        wv = [state["pj_free"][dj % 2]] + list(extra_wait)
        first = True
        for ci in range(NCH):
            for (wop, xop) in ((wh, xh), (wh, xl), (wl, xh)):
                last = (ci == NCH - 1) and (wop is wl)
                op("tensor",
                   lambda e, wop=wop, xop=xop, ci=ci, dj=dj, first=first, last=last:
                   e.matmul(bank[:, 0:TILE], wop[:, ci, dj * 128:(dj + 1) * 128],
                            xop[:, ci, :], start=first, stop=last),
                   wait=wv if first else (), inc=[("T", 1)] if last else ())
                first = False
        return ev("T")

    ctxA = ExitStack()
    wA = {}
    for nm in ("wk", "wq", "wv"):
        wA[nm + "h"] = ctxA.enter_context(nc.sbuf_tensor(nm + "h", [128, NCH, D], BF16))
        wA[nm + "l"] = ctxA.enter_context(nc.sbuf_tensor(nm + "l", [128, NCH, D], BF16))
    x_sb = ctxA.enter_context(nc.sbuf_tensor("x_sb", [128, NCH, TILE], F32))
    qt_scr = ctxA.enter_context(nc.sbuf_tensor("qt_scr", [128, H, CH], F32))
    ln_h = ctxA.enter_context(nc.sbuf_tensor("ln_h", [128, NCH, TILE], BF16))
    ln_l = ctxA.enter_context(nc.sbuf_tensor("ln_l", [128, NCH, TILE], BF16))
    k_sb = ctxA.enter_context(nc.sbuf_tensor("k_sb", [128, H, TILE], F32))
    fq_sb = ctxA.enter_context(nc.sbuf_tensor("fq_sb", [128, 2, H, TILE], F32))
    g_sb = ctxA.enter_context(nc.sbuf_tensor("g_sb", [128, H, TILE], F32))
    xsq_bf = ctxA.enter_context(nc.sbuf_tensor("xsq_bf", [128, NCH, TILE], BF16))
    b_sb = ctxA.enter_context(nc.sbuf_tensor("b_sb", [128, H, CH], F32))
    ebq = ctxA.enter_context(nc.sbuf_tensor("ebq", [128, H, CH], F32))
    ebk = ctxA.enter_context(nc.sbuf_tensor("ebk", [128, H, CH], F32))
    qt_h = ctxA.enter_context(nc.sbuf_tensor("qt_h", [128, 2, H, CH], BF16))
    qt_l = ctxA.enter_context(nc.sbuf_tensor("qt_l", [128, 2, H, CH], BF16))
    kt_h = ctxA.enter_context(nc.sbuf_tensor("kt_h", [128, 2, H, CH], BF16))
    kt_l = ctxA.enter_context(nc.sbuf_tensor("kt_l", [128, 2, H, CH], BF16))
    A_h = ctxA.enter_context(nc.sbuf_tensor("A_h", [128, 512], BF16))
    A_l = ctxA.enter_context(nc.sbuf_tensor("A_l", [128, 512], BF16))
    A2_h = ctxA.enter_context(nc.sbuf_tensor("A2_h", [128, 512], BF16))
    A2_l = ctxA.enter_context(nc.sbuf_tensor("A2_l", [128, 512], BF16))
    osq_bf = xsq_bf[:][:, 0:4, 0:CH]
    o_stage = ctxA.enter_context(nc.sbuf_tensor("o_stage", [128, 4, CH], F32))
    vt_h = ctxA.enter_context(nc.sbuf_tensor("vt_h", [128, 2, H, HD], BF16))
    vt_l = ctxA.enter_context(nc.sbuf_tensor("vt_l", [128, 2, H, HD], BF16))
    kx_h = ctxA.enter_context(nc.sbuf_tensor("kx_h", [128, 3, H, CH], BF16))
    kx_l = ctxA.enter_context(nc.sbuf_tensor("kx_l", [128, 3, H, CH], BF16))
    Ax_h = ctxA.enter_context(nc.sbuf_tensor("Ax_h", [128, 2, 512], BF16))
    Ax_l = ctxA.enter_context(nc.sbuf_tensor("Ax_l", [128, 2, 512], BF16))

    ev_wk = load_split_weight(w_in["wkT"], wA["wkh"], wA["wkl"], eng="vector")
    ev_wq = load_split_weight(w_in["wqT"], wA["wqh"], wA["wql"], first_wait=[ev_wk],
                              eng="gpsimd")
    ev_wv = load_split_weight(w_in["wvT"], wA["wvh"], wA["wvl"], first_wait=[ev_wq],
                              eng="vector")
    bld.flush()

    st = {
        "x_free": ("V", 0),
        "fq_free": [("V", 0), ("V", 0)],
        "k_free": ("G", 0),
        "g_free": ("V", 0),
        "Abank_free": [("V", 0), ("V", 0)],
        "Obank_free": [[("SO0", 0), ("S", 0)], [("SO1", 0), ("S", 0)]],
        "ebq_free": ("V", 0),
        "ebx_done": ("S", 0),
        "ebk_free": ("G", 0),
        "osq_free": ("T", 0),
        "kt_free": [("T", 0), ("T", 0)],
        "vt_free": [("T", 0), ("T", 0)],
        "vt_ev": [("V", 0), ("V", 0)],
        "qt_free": [("T", 0), ("T", 0)],
        "kx_free": [("T", 0), ("T", 0), ("T", 0)],
        "kx_ready": [("G", 0), ("G", 0), ("G", 0)],
        "rms_free": [("V", 0), ("V", 0)],
        "stats_read": ("V", 0),
    }
    at = [dict(ev_g=("S", 0), ev_k=("G", 0), ev_q=("S", 0), ev_ln=("V", 0))
          for _ in range(nt + 1)]
    prep = {}
    n_ostore = 0
    n_rstore = [0]

    def a_prep(gc):
        q, cc = gc // 2, gc % 2
        tsl = slice(cc * CH, (cc + 1) * CH)
        par = gc % 2
        tpar = gc % 3
        fpar = q % 2
        aq = at[q]
        for h in range(H):
            wv = [aq["ev_g"], ("V", C["V"]), st["ebx_done"]] if h == 0 else ()
            op("vector", lambda e, h=h, tsl=tsl: e.tensor_tensor_scan(
                b_sb[:, h, :], g_sb[:, h, tsl], g_sb[:, h, tsl], 0.0, ALU.add, ALU.bypass),
               wait=wv, inc=[("V", 1)], nop=True)
        ev_b = ev("V")
        ev_ebq = op("scalar", lambda e: e.activation(out=ebq[:], in_=b_sb[:], func=AF.Exp,
                                                     bias=b_p44[:, 0:1]),
                    wait=[ev_b, st["ebq_free"], ("S", C["S"])], inc=[("S", 1)])
        ev_ebk = op("scalar", lambda e: e.activation(out=ebk[:], in_=b_sb[:], func=AF.Exp,
                                                     bias=b_m44[:, 0:1], scale=-1.0),
                    wait=[ev_b, st["ebk_free"]], inc=[("S", 1)])
        op("vector", lambda e, tsl=tsl, fpar=fpar: e.tensor_tensor(
            qt_scr[:], fq_sb[:, fpar, :, tsl], ebq[:], ALU.mult),
           wait=[aq["ev_q"], ev_ebq, ("V", C["V"])], inc=[("V", 1)])
        ev_qtmul = ev("V")
        op("vector", lambda e, par=par: e.tensor_copy(qt_h[:, par], qt_scr[:]),
           wait=[st["qt_free"][par], ("V", C["V"])], inc=[("V", 1)])
        op("vector", lambda e, par=par: e.tensor_tensor(qt_l[:, par], qt_scr[:],
                                                        qt_h[:, par], ALU.subtract),
           wait=[("V", C["V"])], inc=[("V", 1)])
        ev_qt = ev("V")
        op("gpsimd", lambda e, tsl=tsl: e.tensor_tensor(ebk[:], k_sb[:, :, tsl], ebk[:],
                                                        ALU.mult),
           wait=[aq["ev_k"], ev_ebk, ("G", C["G"])], inc=[("G", 1)])
        ev_ktmul = ev("G")
        ev_kth = op("scalar", lambda e, par=par: e.activation(
            out=kt_h[:, par], in_=ebk[:], func=AF.Copy),
            wait=[st["kt_free"][par], ev_ktmul, ("S", C["S"])], inc=[("S", 1)])
        op("gpsimd", lambda e, par=par: e.tensor_tensor(kt_l[:, par], ebk[:], kt_h[:, par],
                                                        ALU.subtract),
           wait=[ev_kth, ("G", C["G"])], inc=[("G", 1)])
        ev_kt = ev("G")
        for h in range(H):
            op("scalar", lambda e, h=h: e.activation(
                out=ebq[:, h, :], in_=b_sb[:, h, :], func=AF.Exp, scale=-1.0,
                bias=b_sb[:, h, CH - 1:CH]),
               wait=[ev_qtmul, ev_b, ("S", C["S"])], inc=[("S", 1)])
        ev_ebx = ev("S")
        st["ebx_done"] = ev_ebx
        ev_kxmul = op("gpsimd", lambda e, tsl=tsl: e.tensor_tensor(
            ebq[:], k_sb[:, :, tsl], ebq[:], ALU.mult),
           wait=[ev_ebx, ("G", C["G"])], inc=[("G", 1)])
        ev_kxh = op("scalar", lambda e, tpar=tpar: e.activation(
            out=kx_h[:, tpar], in_=ebq[:], func=AF.Copy),
            wait=[st["kx_free"][tpar], ev_kxmul, ("S", C["S"])], inc=[("S", 1)])
        op("gpsimd", lambda e, tpar=tpar: e.tensor_tensor(kx_l[:, tpar], ebq[:],
                                                          kx_h[:, tpar], ALU.subtract),
           wait=[ev_kxh, ("G", C["G"])], inc=[("G", 1)])
        ev_kx = ev("G")
        st["kx_ready"][tpar] = ev_kx
        st["ebq_free"] = ev_kx
        st["ebk_free"] = ev_kt
        st["k_free"] = ("G", C["G"])
        if cc == 1:
            st["fq_free"][fpar] = ev_qtmul
            st["g_free"] = ev_b
        prep.setdefault(gc, {}).update(
            dict(ev_b=ev_b, ev_qt=ev_qt, ev_kt=ev_kt, par=par, tpar=tpar))

    def a_vproj(gc):
        q, cc = gc // 2, gc % 2
        tsl = slice(cc * CH, (cc + 1) * CH)
        par = gc % 2
        for half in range(2):
            bank = p_pj[half]
            first = True
            for ci in range(NCH):
                for (wop, lnop) in ((wA["wvh"], ln_h), (wA["wvh"], ln_l), (wA["wvl"], ln_h)):
                    last = (ci == NCH - 1) and (wop is wA["wvl"])
                    op("tensor",
                       lambda e, wop=wop, lnop=lnop, ci=ci, half=half, first=first,
                       last=last, tsl=tsl, bank=bank:
                       e.matmul(bank[:], lnop[:, ci, tsl],
                                wop[:, ci, half * 512:(half + 1) * 512],
                                start=first, stop=last),
                       wait=[state["pj_free"][half], st["vt_free"][par], ev_wv]
                       if first else (),
                       inc=[("T", 1)] if last else ())
                    first = False
        ev_vmm = ev("T")
        op("vector", lambda e, par=par: e.tensor_copy(vt_h[:, par, 0:4, :], p_pj[0][:]),
           wait=[ev_vmm, ("V", C["V"])], inc=[("V", 1)])
        op("vector", lambda e, par=par: e.tensor_copy(vt_h[:, par, 4:8, :], p_pj[1][:]),
           wait=[("V", C["V"])], inc=[("V", 1)])
        op("vector", lambda e, par=par: e.tensor_tensor(
            vt_l[:, par, 0:4, :], p_pj[0][:], vt_h[:, par, 0:4, :], ALU.subtract),
           wait=[("V", C["V"])], inc=[("V", 1)])
        op("vector", lambda e, par=par: e.tensor_tensor(
            vt_l[:, par, 4:8, :], p_pj[1][:], vt_h[:, par, 4:8, :], ALU.subtract),
           wait=[("V", C["V"])], inc=[("V", 1)])
        ev_vev = ev("V")
        st["vt_ev"][par] = ev_vev
        state["pj_free"][0] = ev_vev
        state["pj_free"][1] = ev_vev
        prep.setdefault(gc, {})["ev_vev"] = ev_vev

    def a_gla(gc):
        q, cc = gc // 2, gc % 2
        tsl = slice(cc * CH, (cc + 1) * CH)
        p_ = prep[gc]
        par = p_["par"]
        xpar = (p_["tpar"] + 2) % 3
        first_chunk = (gc == 0)
        ev_vev = p_["ev_vev"]
        qh, ql = qt_h[:, par], qt_l[:, par]
        kh, kl = kt_h[:, par], kt_l[:, par]
        ev_Amm = [None, None]
        for g4 in range(2):
            hs = [4 * g4 + i for i in range(4)]
            bank = p_A if g4 == 0 else p_A2
            bfree = st["Abank_free"][g4]
            first = True
            for h in hs:
                for (kop, qop) in ((kh, qh), (kh, ql), (kl, qh)):
                    last = (h == hs[-1]) and (kop is kl)
                    op("tensor",
                       lambda e, kop=kop, qop=qop, h=h, first=first, last=last,
                       g4=g4, bank=bank:
                       e.matmul(bank[:, (h - 4 * g4) * 128:(h - 4 * g4 + 1) * 128],
                                kop[:, h, :], qop[:, h, :], start=first, stop=last),
                       wait=[p_["ev_kt"], p_["ev_qt"], bfree] if first else (),
                       inc=[("T", 1)] if last else ())
                    first = False
            ev_Amm[g4] = ev("T")
        ev_Aev = [None, None]
        for g4 in range(2):
            bank = p_A if g4 == 0 else p_A2
            Ah = A_h if g4 == 0 else A2_h
            Al = A_l if g4 == 0 else A2_l
            op("vector", lambda e, bank=bank: e.tensor_tensor(
                bank[:], bank[:], mask4[:], ALU.mult),
               wait=[ev_Amm[g4], ("V", C["V"])], inc=[("V", 1)])
            op("vector", lambda e, bank=bank, Ah=Ah: e.tensor_copy(Ah[:], bank[:]),
               wait=[("V", C["V"])], inc=[("V", 1)])
            op("vector", lambda e, bank=bank, Ah=Ah, Al=Al: e.tensor_tensor(
                Al[:], bank[:], Ah[:], ALU.subtract),
               wait=[("V", C["V"])], inc=[("V", 1)])
            ev_Aev[g4] = ev("V")
            st["Abank_free"][g4] = ev_Aev[g4]
        if not first_chunk:
            ev_Axmm = [None, None]
            for g4 in range(2):
                hs = [4 * g4 + i for i in range(4)]
                bank = p_A if g4 == 0 else p_A2
                first = True
                for h in hs:
                    for (kop, qop) in ((kx_h, qh), (kx_h, ql), (kx_l, qh)):
                        last = (h == hs[-1]) and (kop is kx_l)
                        op("tensor",
                           lambda e, kop=kop, qop=qop, h=h, first=first, last=last,
                           g4=g4, bank=bank, xpar=xpar:
                           e.matmul(bank[:, (h - 4 * g4) * 128:(h - 4 * g4 + 1) * 128],
                                    kop[:, xpar, h, :], qop[:, h, :],
                                    start=first, stop=last),
                           wait=[st["kx_ready"][xpar], p_["ev_qt"], ev_Aev[g4]]
                           if first else (),
                           inc=[("T", 1)] if last else ())
                        first = False
                ev_Axmm[g4] = ev("T")
            st["kx_free"][xpar] = ev_Axmm[1]
            ev_Axev = [None, None]
            for g4 in range(2):
                bank = p_A if g4 == 0 else p_A2
                op("vector", lambda e, bank=bank, g4=g4: e.tensor_scalar_mul(
                    Ax_h[:, g4, :], bank[:], E_MSH),
                   wait=[ev_Axmm[g4], ("V", C["V"])], inc=[("V", 1)])
                op("vector", lambda e, bank=bank, g4=g4: e.scalar_tensor_tensor(
                    Ax_l[:, g4, :], bank[:], E_MSH, Ax_h[:, g4, :], ALU.mult,
                    ALU.subtract),
                   wait=[("V", C["V"])], inc=[("V", 1)])
                ev_Axev[g4] = ev("V")
                st["Abank_free"][g4] = ev_Axev[g4]

        ev_rms_mm = None
        for g4 in range(2):
            hs = [4 * g4 + i for i in range(4)]
            Ah = A_h if g4 == 0 else A2_h
            Al = A_l if g4 == 0 else A2_l
            obank = p_O if g4 == 0 else p_O2
            first = True
            for hi_, h in enumerate(hs):
                for pi, (vop, aop) in enumerate(((vt_h, Ah), (vt_h, Al), (vt_l, Ah))):
                    last = first_chunk and (hi_ == 3) and (pi == 2)
                    op("tensor",
                       lambda e, vop=vop, aop=aop, h=h, g4=g4, first=first, last=last,
                       obank=obank, par=par:
                       e.matmul(obank[:, (h - 4 * g4) * 128:(h - 4 * g4 + 1) * 128],
                                vop[:, par, h, :],
                                aop[:, (h - 4 * g4) * 128:(h - 4 * g4 + 1) * 128],
                                start=first, stop=last),
                       wait=[ev_Aev[g4], ev_vev] + st["Obank_free"][g4]
                       if first else (),
                       inc=[("T", 1)] if last else ())
                    first = False
            if not first_chunk:
                ppar = 1 - par
                for hi_, h in enumerate(hs):
                    for pi, (vop, axop) in enumerate(((vt_h, Ax_h), (vt_h, Ax_l),
                                                      (vt_l, Ax_h))):
                        last = (hi_ == 3) and (pi == 2)
                        op("tensor",
                           lambda e, vop=vop, axop=axop, h=h, g4=g4, last=last,
                           obank=obank, ppar=ppar:
                           e.matmul(obank[:, (h - 4 * g4) * 128:(h - 4 * g4 + 1) * 128],
                                    vop[:, ppar, h, :],
                                    axop[:, g4, (h - 4 * g4) * 128:(h - 4 * g4 + 1) * 128],
                                    start=False, stop=last),
                           wait=[ev_Axev[g4], st["vt_ev"][ppar]]
                           if (hi_ == 0 and pi == 0) else (),
                           inc=[("T", 1)] if last else ())
            ev_Omm = ev("T")
            if g4 == 1:
                st["qt_free"][par] = ev_Omm
                st["kt_free"][par] = ev_Amm[1]
                if not first_chunk:
                    st["vt_free"][1 - par] = ev_Omm
                else:
                    st["vt_free"][1 - par] = ("T", C["T"])
            so = "SO0" if g4 == 0 else "SO1"
            ev_oev = op("scalar", lambda e, obank=obank: e.activation(
                out=o_stage[:], in_=obank[:].rearrange("p (a c) -> p a c", c=128),
                func=AF.Copy),
                wait=[ev_Omm, ("SO0", C["SO0"]), ("SO1", C["SO1"]), ("S", C["S"])],
                inc=[("S", 1)])
            ev_osq = op("scalar", lambda e: e.activation(
                out=osq_bf, in_=o_stage[:], func=AF.Square),
                wait=[ev_oev, st["osq_free"], ("S", C["S"])], inc=[("S", 1)])
            for i in range(4):
                op("tensor", lambda e, i=i, cc=cc, g4=g4: e.matmul(
                    p_st[64:65, 256 + cc * CH:256 + (cc + 1) * CH], onescb[:],
                    osq_bf[:, i, :], start=(g4 == 0 and i == 0),
                    stop=(g4 == 1 and i == 3)),
                   wait=[ev_osq, st["rms_free"][cc]] if i == 0 else (),
                   inc=[("T", 1)] if (g4 == 1 and i == 3) else ())
            st["osq_free"] = ("T", C["T"])
            ev_rms_mm = ("T", C["T"])
            op("sync", lambda e, g4=g4, q=q, cc=cc: e.dma_start(
                out=o_scr[:, q, 4 * g4:4 * g4 + 4, cc * CH:(cc + 1) * CH],
                in_=o_stage[:]),
               wait=[ev_oev], inc=[(so, 16)])
            st["Obank_free"][g4] = [("S", C["S"])]

        op("vector", lambda e, cc=cc: e.tensor_scalar_mul(
            r_rms[:, cc * CH:(cc + 1) * CH],
            p_st[64:65, 256 + cc * CH:256 + (cc + 1) * CH], INV_D),
           wait=[ev_rms_mm, ("SR", n_rstore[0] * 16), ("V", C["V"])], inc=[("V", 1)])
        op("scalar", lambda e, cc=cc: e.activation(
            out=r_rms[:, cc * CH:(cc + 1) * CH], in_=r_rms[:, cc * CH:(cc + 1) * CH],
            func=AF.Sqrt, bias=eps6[:, 0:1]),
           wait=[("V", C["V"]), ("S", C["S"])], inc=[("S", 1)])
        op("vector", lambda e, cc=cc: e.reciprocal(
            r_rms[:, cc * CH:(cc + 1) * CH], r_rms[:, cc * CH:(cc + 1) * CH]),
           wait=[("S", C["S"])], inc=[("V", 1)], nop=True)
        st["rms_free"][cc] = ev("V")
        if cc == 1:
            op("sync", lambda e, q=q: e.dma_start(
                out=rms_scr[0:1, q * TILE:q * TILE + TILE], in_=r_rms[:]),
               wait=[st["rms_free"][0], st["rms_free"][1]], inc=[("SR", 16)])
            n_rstore[0] += 1

    def a_stats(p):
        ts0 = p * TILE
        op("sync", lambda e, p=p: e.dma_start(out=x_sb[:], in_=xT[:, p]),
           wait=[st["x_free"]], inc=[("LX", 16)])
        ev_x = ev("LX")
        ev_sq = op("scalar", lambda e: e.activation(out=xsq_bf[:], in_=x_sb[:], func=AF.Square),
                   wait=[ev_x, ("T", C["T"]), state["stats_free"], ("S", C["S"])],
                   inc=[("S", 1)])
        stats_reduce(x_sb, xsq_bf[:], first_wait=[ev_x, ev_sq])
        row_stats(eps5)
        state["stats_free"] = ev("V")
        at[p]["ev_x"] = ev_x

    def a_bc(p):
        state["bc_ev"] = bcast_rows()

    def a_ln(p):
        op("vector", lambda e: e.tensor_tensor(x_sb[:], x_sb[:], bcA(), ALU.mult),
           wait=[state["bc_ev"], ("V", C["V"])], inc=[("V", 1)])
        op("vector", lambda e: e.tensor_tensor(x_sb[:], x_sb[:], bcB(), ALU.add),
           wait=[("V", C["V"])], inc=[("V", 1)])
        state["bc_read"] = ev("V")
        op("vector", lambda e: e.tensor_copy(ln_h[:], x_sb[:]),
           wait=[("V", C["V"])], inc=[("V", 1)])
        op("vector", lambda e: e.tensor_tensor(ln_l[:], x_sb[:], ln_h[:], ALU.subtract),
           wait=[("V", C["V"])], inc=[("V", 1)])
        ev_ln = ev("V")
        at[p]["ev_ln"] = ev_ln
        st["x_free"] = ev_ln

    def a_wk(p):
        fpar = p % 2
        ev_fev = None
        for dj in range(NCH):
            evt = proj_feature(wA["wkh"], wA["wkl"], ln_h, ln_l, dj,
                               extra_wait=[at[p]["ev_ln"], st["fq_free"][fpar], ev_wk]
                               if dj < 2 else [])
            ev_fev = op("scalar", lambda e, dj=dj, fpar=fpar: e.activation(
                out=fq_sb[:, fpar, dj, :], in_=p_pj[dj % 2][:, 0:TILE], func=AF.Sigmoid),
                wait=[evt], inc=[("S", 1)])
            state["pj_free"][dj % 2] = ev_fev
        wv = [ev_fev, st["g_free"], ("S", C["S"])]
        op("scalar", lambda e, fpar=fpar: e.activation(out=g_sb[:], in_=fq_sb[:, fpar],
                                                       func=AF.Sigmoid),
           wait=wv, inc=[("S", 1)])
        op("scalar", lambda e: e.activation(out=g_sb[:], in_=g_sb[:], func=AF.Ln),
           wait=[("S", C["S"])], inc=[("S", 1)])
        at[p]["ev_g"] = ev("S")
        at[p]["ev_k"] = op("gpsimd", lambda e, fpar=fpar: e.tensor_scalar(
            out=k_sb[:], in0=fq_sb[:, fpar], scalar1=-1.0, scalar2=1.0,
            op0=ALU.mult, op1=ALU.add),
            wait=[ev_fev, st["k_free"]], inc=[("G", 1)])

    def a_wq(p):
        fpar = p % 2
        ev_qev = None
        for dj in range(NCH):
            evt = proj_feature(wA["wqh"], wA["wql"], ln_h, ln_l, dj,
                               extra_wait=[ev_wq] if dj < 2 else [])
            ev_qev = op("scalar", lambda e, dj=dj, fpar=fpar: e.activation(
                out=fq_sb[:, fpar, dj, :], in_=p_pj[dj % 2][:, 0:TILE], func=AF.Silu),
                wait=[evt, at[p]["ev_g"], at[p]["ev_k"]] if dj == 0 else [evt],
                inc=[("S", 1)])
            state["pj_free"][dj % 2] = ev_qev
        at[p]["ev_q"] = ev_qev
        st["ln_free"] = ("T", C["T"])

    a_stats(0)
    for it in range(nt + 1):
        p, q = it, it - 1
        if p < nt:
            a_bc(p)
        if q >= 0:
            a_prep(2 * q)
            a_prep(2 * q + 1)
            a_vproj(2 * q)
            a_gla(2 * q)
            a_vproj(2 * q + 1)
        if p < nt:
            a_ln(p)
            a_wk(p)
        if q >= 0:
            a_gla(2 * q + 1)
        if p < nt:
            a_wq(p)
            if p + 1 < nt:
                a_stats(p + 1)
        bld.flush()

    n_rstore = n_rstore[0]
    ev_A_drain = [(s, C[s]) for s in ("V", "S", "G", "T")] + \
                 [("SO0", C["SO0"]), ("SO1", C["SO1"]), ("SR", n_rstore * 16),
                  ("LX", C["LX"])]
    bld.flush()
    ctxA.close()


    ctxB = ExitStack()
    wB = {}
    for nm in ("wo", "wr", "wk2", "wv2"):
        wB[nm + "h"] = ctxB.enter_context(nc.sbuf_tensor(nm + "h", [128, NCH, D], BF16))
        wB[nm + "l"] = ctxB.enter_context(nc.sbuf_tensor(nm + "l", [128, NCH, D], BF16))
    xB = ctxB.enter_context(nc.sbuf_tensor("xB", [128, NCH, TILE], F32))
    x2_sb = ctxB.enter_context(nc.sbuf_tensor("x2_sb", [128, NCH, TILE], F32))
    oB = ctxB.enter_context(nc.sbuf_tensor("oB", [128, NCH, TILE], F32))
    oB_bf = oB[:].bitcast(BF16)
    sqA = oB_bf[:, :, 0:TILE]
    sqB = oB_bf[:, :, TILE:2 * TILE]
    o_pair = ctxB.enter_context(nc.sbuf_tensor("o_pair", [128, NCH, 2, TILE], BF16))
    o_h = o_pair[:][:, :, 0, :]
    o_l = o_pair[:][:, :, 1, :]

    r_sb = ctxB.enter_context(nc.sbuf_tensor("r_sb", [128, NCH, TILE], F32))
    ffv = r_sb[:]
    ln2 = ctxB.enter_context(nc.sbuf_tensor("ln2", [128, NCH, TILE + 1], F32))
    scr1 = ctxB.enter_context(nc.sbuf_tensor("scr1", [128, NCH, TILE], F32))
    scr2 = ctxB.enter_context(nc.sbuf_tensor("scr2", [128, NCH, TILE], F32))
    scr2_bf = scr2[:].bitcast(BF16)[:, :, 0:TILE]
    xr_h = ctxB.enter_context(nc.sbuf_tensor("xr_h", [128, NCH, TILE], BF16))
    xr_l = ctxB.enter_context(nc.sbuf_tensor("xr_l", [128, NCH, TILE], BF16))
    xk_h = ctxB.enter_context(nc.sbuf_tensor("xk_h", [128, NCH, TILE], BF16))
    xk_l = ctxB.enter_context(nc.sbuf_tensor("xk_l", [128, NCH, TILE], BF16))
    kc_h = xk_h
    kc_l = xk_l


    TZ, VZ, SZ = ("T", 0), ("V", 0), ("S", 0)
    bt = [dict(xk=VZ, x2=VZ, x3=VZ, WoT=TZ, WrT=TZ, Wv2T=TZ, ff=VZ, kc=VZ, aoRM=VZ,
               ffstat=TZ, aostat=TZ, sqB_T=TZ, revac=SZ, ffsq=SZ) for _ in range(nt + 1)]
    state["pj_free"] = [("V", 0), ("V", 0)]
    n_bstore = [0]

    def b_wr(q):
        me = bt[q]
        ev_rev = None
        for dj in range(NCH):
            evt = proj_feature(wB["wrh"], wB["wrl"], xr_h, xr_l, dj,
                               extra_wait=[me["xk"], ev_wr] if dj < 2 else [])
            ev_rev = op("scalar", lambda e, dj=dj: e.activation(
                out=r_sb[:, dj, :], in_=p_pj[dj % 2][:, 0:TILE], func=AF.Sigmoid),
                wait=[evt, bt[q - 1]["x3"] if q > 0 else VZ,
                      bt[q - 1]["ffstat"] if q > 0 else TZ, ("S", C["S"])]
                if dj == 0 else [evt],
                inc=[("S", 1)])
            state["pj_free"][dj % 2] = ev_rev
        me["WrT"] = ("T", C["T"])
        me["revac"] = ev("S")
        me["r"] = ev_rev

    def b_load_dma(p):
        ts0 = p * TILE
        pv = bt[p - 1] if p > 0 else bt[nt]
        op("sync", lambda e, p=p: e.dma_start(out=oB[:], in_=o_scr[:, p]),
           wait=[("SO0", C["SO0"]), ("SO1", C["SO1"]), pv["sqB_T"], pv["aostat"]],
           inc=[("LO", 16)])
        op("sync", lambda e, ts0=ts0: e.dma_start(out=r_rr[:], in_=rms_scr[0:1, ts0:ts0 + TILE]),
           wait=[("SR", n_rstore * 16), pv["aoRM"]], inc=[("LR", 16)])
        if p == 0:
            op("sync", lambda e: e.dma_start(out=xB[:], in_=xT[:, 0]),
               wait=[], inc=[("LX", 16)])

    def b_load_wo(p):
        pv = bt[p - 1] if p > 0 else bt[nt]
        me = bt[p]
        if p > 0:
            b_load_dma(p)
        ev_o = ev("LO")
        ev_rrow = ev("LR")
        me["x_ld"] = ("LX", C["LX"])
        ev_oh = op("scalar", lambda e: e.activation(out=o_h, in_=oB[:], func=AF.Copy),
                   wait=[ev_o, pv["WoT"], pv["revac"], ("S", C["S"])], inc=[("S", 1)])
        op("gpsimd", lambda e: e.tensor_tensor(o_l, oB[:], o_h, ALU.subtract),
           wait=[ev_oh, ("G", C["G"])], inc=[("G", 1)])
        ev_ospl = ev("G")
        ev_aoev = None
        for dj in range(NCH):
            evt = proj_feature(wB["woh"], wB["wol"], o_h, o_l, dj,
                               extra_wait=[ev_ospl, ev_wo] if dj < 2 else [])
            ev_aoev = op("scalar", lambda e, dj=dj: e.activation(
                out=scr1[:, dj, :], in_=p_pj[dj % 2][:, 0:TILE], func=AF.Copy),
                wait=[evt, pv["xk"], ("S", C["S"])], inc=[("S", 1)])
            state["pj_free"][dj % 2] = ev_aoev
        me["WoT"] = ("T", C["T"])
        ev_sq = op("scalar", lambda e: e.activation(out=sqA, in_=scr1[:], func=AF.Square),
                   wait=[ev_aoev, ("LO", C["LO"]), state["stats_free"], ("S", C["S"])],
                   inc=[("S", 1)])
        stats_reduce(scr1, sqA, first_wait=[ev_aoev, ev_sq])
        me["aostat"] = ("T", C["T"])
        op("vector", lambda e: e.tensor_scalar_mul(r_mean[:], p_st[0:1, 0:TILE], INV_D),
           wait=[("T", C["T"]), ("V", C["V"])], inc=[("V", 1)])
        op("vector", lambda e: e.tensor_tensor(r_msq[:], r_mean[:], r_mean[:], ALU.mult),
           wait=[("V", C["V"])], inc=[("V", 1)])
        op("vector", lambda e: e.scalar_tensor_tensor(
            r_var[:], p_st[32:33, 0:TILE], INV_D, r_msq[:], ALU.mult, ALU.subtract),
           wait=[("V", C["V"])], inc=[("V", 1)])
        op("vector", lambda e: e.tensor_tensor(r_t0[:], r_rr[:], r_rr[:], ALU.mult),
           wait=[ev_rrow, ("V", C["V"])], inc=[("V", 1)])
        op("vector", lambda e: e.tensor_tensor(r_var[:], r_var[:], r_t0[:], ALU.mult),
           wait=[("V", C["V"])], inc=[("V", 1)])
        op("scalar", lambda e: e.activation(out=r_t0[:], in_=r_var[:], func=AF.Sqrt,
                                            bias=eps5[:, 0:1]),
           wait=[("V", C["V"]), ("S", C["S"])], inc=[("S", 1)])
        op("vector", lambda e: e.reciprocal(r_var[:], r_t0[:]),
           wait=[("S", C["S"])], inc=[("V", 1)], nop=True)
        op("vector", lambda e: e.tensor_tensor(r_a[:], r_var[:], r_rr[:], ALU.mult),
           wait=[("V", C["V"])], inc=[("V", 1)])
        op("vector", lambda e: e.scalar_tensor_tensor(
            r_b[:], r_mean[:], -1.0, r_a[:], ALU.mult, ALU.mult),
           wait=[("V", C["V"])], inc=[("V", 1)])
        state["stats_free"] = ev("V")
        me["aoRM"] = ev("V")

    def b_wk2(q):
        me = bt[q]
        ev_kev = None
        for dj in range(NCH):
            evt = proj_feature(wB["wk2h"], wB["wk2l"], xk_h, xk_l, dj,
                               extra_wait=[ev_wk2, me["xk"]] if dj < 2 else [])
            ev_kev = op("scalar", lambda e, dj=dj: e.activation(
                out=scr2[:, dj, :], in_=p_pj[dj % 2][:, 0:TILE], func=AF.Relu),
                wait=[evt, bt[q - 1]["ffstat"] if q > 0 else TZ, ("S", C["S"])]
                if dj == 0 else [evt],
                inc=[("S", 1)])
            state["pj_free"][dj % 2] = ev_kev
        ev_wk2T = ("T", C["T"])
        op("vector", lambda e: e.tensor_tensor(scr2[:], scr2[:], scr2[:], ALU.mult),
           wait=[ev_kev, ("V", C["V"])], inc=[("V", 1)])
        op("vector", lambda e: e.tensor_copy(kc_h[:], scr2[:]),
           wait=[("V", C["V"]), ev_wk2T], inc=[("V", 1)])
        op("vector", lambda e: e.tensor_tensor(kc_l[:], scr2[:], kc_h[:], ALU.subtract),
           wait=[("V", C["V"])], inc=[("V", 1)])
        me["kc"] = ev("V")

    def b_wv2(q):
        me = bt[q]
        ev_ffev = None
        for dj in range(NCH):
            evt = proj_feature(wB["wv2h"], wB["wv2l"], kc_h, kc_l, dj,
                               extra_wait=[me["kc"], ev_wv2] if dj < 2 else [])
            ev_ffev = op("vector", lambda e, dj=dj: e.tensor_tensor(
                ffv[:, dj, :], p_pj[dj % 2][:, 0:TILE], r_sb[:, dj, :], ALU.mult),
                wait=[evt, me["r"], bt[q - 1]["ffsq"] if q > 0 else SZ,
                      bt[q - 1]["ffstat"] if q > 0 else TZ, ("V", C["V"])],
                inc=[("V", 1)])
            state["pj_free"][dj % 2] = ev_ffev
        me["Wv2T"] = ("T", C["T"])
        me["ff"] = ev_ffev

    def b_aobc(p):
        me = bt[p]
        bcast_rows()
        me["aobc"] = ("T", C["T"])
        op("vector", lambda e: e.tensor_tensor(scr1[:], scr1[:], bcA(), ALU.mult),
           wait=[me["aobc"], ("V", C["V"])], inc=[("V", 1)])
        op("vector", lambda e: e.tensor_tensor(scr1[:], scr1[:], bcB(), ALU.add),
           wait=[("V", C["V"])], inc=[("V", 1)])
        state["bc_read"] = ev("V")

    def b_ffstat(q):
        me = bt[q]
        ev_sq = op("scalar", lambda e: e.activation(out=scr2_bf, in_=ffv, func=AF.Square),
                   wait=[me["ff"], me["kc"], state["stats_free"], ("S", C["S"])],
                   inc=[("S", 1)])
        me["ffsq"] = ev_sq
        stats_reduce(ffv, scr2_bf, first_wait=[me["ff"], ev_sq])
        me["ffstat"] = ("T", C["T"])
        row_stats(eps5)
        state["stats_free"] = ev("V")

    def b_fin(q):
        ts0 = q * TILE
        me = bt[q]
        bcast_rows()
        op("vector", lambda e: e.tensor_tensor(ffv, ffv, bcA(), ALU.mult),
           wait=[("T", C["T"]), ("V", C["V"])], inc=[("V", 1)])
        op("vector", lambda e: e.tensor_tensor(ffv, ffv, bcB(), ALU.add),
           wait=[("V", C["V"])], inc=[("V", 1)])
        state["bc_read"] = ev("V")
        op("vector", lambda e: e.tensor_tensor(x2_sb[:], x2_sb[:], ffv, ALU.add),
           wait=[("V", C["V"]), ("SX", C["SX"])], inc=[("V", 1)])
        me["x3"] = ev("V")
        if ts0 >= warm:
            op("sync", lambda e, ts0=ts0: e.dma_start(
                out=outTc[:, :, ts0 - warm:ts0 - warm + TILE], in_=x2_sb[:]),
               wait=[me["x3"]], inc=[("SB", 16)])
            n_bstore[0] += 1

    def b_x2(p):
        ts0 = p * TILE
        me = bt[p]
        op("vector", lambda e: e.tensor_tensor(x2_sb[:], xB[:], scr1[:], ALU.add),
           wait=[me["x_ld"], ("SB", n_bstore[0] * 16), ("V", C["V"])], inc=[("V", 1)])
        me["x2"] = ev("V")
        if debug:
            op("sync", lambda e, ts0=ts0: e.dma_start(
                out=x2_scr.rearrange("(c p) n -> p c n", p=128)[:, :, ts0:ts0 + TILE],
                in_=x2_sb[:]), wait=[me["x2"]], inc=[("SX", 16)])
        if p + 1 < nt:
            op("sync", lambda e, p=p: e.dma_start(out=xB[:], in_=xT[:, p + 1]),
               wait=[me["x2"]], inc=[("LX", 16)])

    def b_ln2_xrxk(p):
        me = bt[p]
        pv = bt[p - 1] if p > 0 else bt[nt]
        ev_sq = op("scalar", lambda e: e.activation(out=sqB, in_=x2_sb[:], func=AF.Square),
                   wait=[me["x2"], ("LO", C["LO"]), ("S", C["S"]), state["stats_free"]],
                   inc=[("S", 1)])
        stats_reduce(x2_sb, sqB, first_wait=[me["x2"], ev_sq])
        me["sqB_T"] = ("T", C["T"])
        row_stats(eps5)
        state["stats_free"] = ev("V")
        bcast_rows()
        op("vector", lambda e: e.tensor_copy(ln2[:, :, 0:1], ln2[:, :, TILE:TILE + 1]),
           wait=[("V", C["V"]), ev_ln2init], inc=[("V", 1)])
        op("vector", lambda e: e.tensor_tensor(ln2[:, :, 1:TILE + 1], x2_sb[:], bcA(),
                                               ALU.mult),
           wait=[("T", C["T"]), ("V", C["V"])], inc=[("V", 1)])
        op("vector", lambda e: e.tensor_tensor(ln2[:, :, 1:TILE + 1], ln2[:, :, 1:TILE + 1],
                                               bcB(), ALU.add),
           wait=[("V", C["V"])], inc=[("V", 1)])
        state["bc_read"] = ev("V")
        ev_ln2 = ev("V")
        op("vector", lambda e: e.tensor_tensor(scr1[:], ln2[:, :, 1:TILE + 1],
                                               ln2[:, :, 0:TILE], ALU.subtract),
           wait=[ev_ln2], inc=[("V", 1)])
        mur_bc = mur[:].unsqueeze(2).broadcast_to([128, NCH, TILE])
        muk_bc = muk[:].unsqueeze(2).broadcast_to([128, NCH, TILE])
        q = p - 1
        gate_scr2 = bt[q]["ffstat"] if q >= 0 else TZ
        gate_xr = bt[q]["WrT"] if q >= 0 else TZ
        gate_xk = bt[q]["Wv2T"] if q >= 0 else TZ
        op("vector", lambda e: e.tensor_tensor(scr2[:], scr1[:], mur_bc, ALU.mult),
           wait=[("V", C["V"]), gate_scr2, ("S", C["S"])], inc=[("V", 1)])
        op("vector", lambda e: e.tensor_tensor(scr2[:], scr2[:], ln2[:, :, 0:TILE], ALU.add),
           wait=[("V", C["V"])], inc=[("V", 1)])
        op("vector", lambda e: e.tensor_copy(xr_h[:], scr2[:]),
           wait=[("V", C["V"]), gate_xr], inc=[("V", 1)])
        op("vector", lambda e: e.tensor_tensor(xr_l[:], scr2[:], xr_h[:], ALU.subtract),
           wait=[("V", C["V"])], inc=[("V", 1)])
        op("vector", lambda e: e.tensor_tensor(scr2[:], scr1[:], muk_bc, ALU.mult),
           wait=[("V", C["V"])], inc=[("V", 1)])
        op("vector", lambda e: e.tensor_tensor(scr2[:], scr2[:], ln2[:, :, 0:TILE], ALU.add),
           wait=[("V", C["V"])], inc=[("V", 1)])
        op("vector", lambda e: e.tensor_copy(xk_h[:], scr2[:]),
           wait=[("V", C["V"]), gate_xk], inc=[("V", 1)])
        op("vector", lambda e: e.tensor_tensor(xk_l[:], scr2[:], xk_h[:], ALU.subtract),
           wait=[("V", C["V"])], inc=[("V", 1)])
        me["xk"] = ev("V")

    b_load_dma(0)
    ev_wo = load_split_weight(w_in["woT"], wB["woh"], wB["wol"], scale_nw=True,
                              first_wait=ev_A_drain, eng="vector")
    op("vector", lambda e: e.memset(ln2[:], 0.0), inc=[("V", 1)], nop=True)
    ev_ln2init = ev("V")
    b_load_wo(0)
    b_aobc(0)
    b_x2(0)
    b_ln2_xrxk(0)
    bld.flush()
    ev_wr = load_split_weight(w_in["wrT"], wB["wrh"], wB["wrl"], first_wait=[ev_wo],
                              eng="gpsimd")
    ev_wk2 = load_split_weight(w_in["wk2T"], wB["wk2h"], wB["wk2l"], first_wait=[ev_wr],
                               eng="vector")
    ev_wv2 = load_split_weight(w_in["wv2T"], wB["wv2h"], wB["wv2l"], first_wait=[ev_wk2],
                               eng="gpsimd")
    bld.flush()
    for it in range(1, nt + 1):
        p, q = it, it - 1
        if p < nt:
            b_load_wo(p)
        if q >= 0:
            b_wr(q)
            b_wk2(q)
            b_wv2(q)
        if p < nt:
            b_aobc(p)
        if q >= 0:
            b_ffstat(q)
            b_fin(q)
        if p < nt:
            b_x2(p)
            b_ln2_xrxk(p)
        bld.flush()
    n_bstore = n_bstore[0]


    bld.wait("sync", ("SB", n_bstore * 16), ("SO0", C["SO0"]), ("SO1", C["SO1"]),
             ("SR", n_rstore * 16), ("SX", C["SX"]))
    bld.flush()
    ctxB.close()
    ctx.close()
    return nc


_WARM = 256
_SLICE = 4096


def _consts():
    import ml_dtypes
    tri = np.triu(np.ones((128, 128), dtype=np.float32))
    mask4 = np.tile(tri, (1, 4)).astype(ml_dtypes.bfloat16)
    return {
        "identv": np.eye(128, dtype=np.float32),
        "mask4v": mask4,
        "onescv": np.ones((128, 1), dtype=np.float32),
        "onesrv": np.ones((1, 128), dtype=np.float32),
    }


def _weight_maps(Wq, Wk, Wv, Wo, norm_w, W_r, W_k, W_v, mu_r, mu_k):
    m = {
        "wqT": np.ascontiguousarray(np.asarray(Wq, np.float32).T),
        "wkT": np.ascontiguousarray(np.asarray(Wk, np.float32).T),
        "wvT": np.ascontiguousarray(np.asarray(Wv, np.float32).T),
        "woT": np.ascontiguousarray(np.asarray(Wo, np.float32).T),
        "wrT": np.ascontiguousarray(np.asarray(W_r, np.float32).T),
        "wk2T": np.ascontiguousarray(np.asarray(W_k, np.float32).T),
        "wv2T": np.ascontiguousarray(np.asarray(W_v, np.float32).T),
        "nwv": np.ascontiguousarray(np.asarray(norm_w, np.float32).reshape(NCH, 128).T),
        "murv": np.ascontiguousarray(np.asarray(mu_r, np.float32).reshape(NCH, 128).T),
        "mukv": np.ascontiguousarray(np.asarray(mu_k, np.float32).reshape(NCH, 128).T),
    }
    m.update(_consts())
    return m


def _xt_block(sl):
    tok = sl.shape[0]
    return np.ascontiguousarray(
        sl.T.reshape(NCH, 128, tok // TILE, TILE).transpose(1, 2, 0, 3))


def kernel(L0_attn_x, L0_attn_Wq, L0_attn_Wk, L0_attn_Wv, L0_attn_Wo, L0_attn_norm_w,
           L0_ffn_W_r, L0_ffn_W_k, L0_ffn_W_v, L0_ffn_mu_r, L0_ffn_mu_k):
    x = np.asarray(L0_attn_x, dtype=np.float32)
    B, N, _ = x.shape
    tok = _WARM + _SLICE

    wmats = _weight_maps(L0_attn_Wq, L0_attn_Wk, L0_attn_Wv, L0_attn_Wo, L0_attn_norm_w,
                         L0_ffn_W_r, L0_ffn_W_k, L0_ffn_W_v, L0_ffn_mu_r, L0_ffn_mu_k)

    in_maps = []
    for core in range(8):
        b, j = core // 4, core % 4
        lo = j * _SLICE - _WARM
        if lo < 0:
            sl = np.concatenate([np.zeros((_WARM, D), np.float32),
                                 x[b, 0:(j + 1) * _SLICE]], axis=0)
        else:
            sl = x[b, lo:(j + 1) * _SLICE]
        m = dict(wmats)
        m["xT"] = _xt_block(sl)
        in_maps.append(m)

    nc = build_program(tok, _WARM)
    res = run_bass_kernel_spmd(nc, in_maps, core_ids=list(range(8)))
    out = np.empty((2, 16384, D), dtype=np.float32)
    for core in range(8):
        b, j = core // 4, core % 4
        out[b, j * _SLICE:(j + 1) * _SLICE] = res.results[core]["outT"].T
    return out


# version 28
# speedup vs baseline: 1.0330x; 1.0330x over previous
import numpy as np
import concourse.bass as bass
import concourse.mybir as mybir
from concourse.bass_utils import run_bass_kernel_spmd
from contextlib import ExitStack

F32 = mybir.dt.float32
BF16 = mybir.dt.bfloat16
AF = mybir.ActivationFunctionType
ALU = mybir.AluOpType

D = 1024
NCH = 8
H = 8
HD = 128
CH = 128
TILE = 256
MSH = 44.0
EPS_LN = 1e-5
EPS_RMS = 1e-6
INV_D = 1.0 / D
E_MSH = float(np.exp(-MSH))


class Bld:
    ENGS = ("sync", "vector", "scalar", "gpsimd", "tensor")

    def __init__(self, nc, ctx, sem_names):
        self.nc = nc
        self.block = ctx.enter_context(nc.Block())
        self.sem = {}
        self.cnt = {}
        for name in sem_names:
            self.sem[name] = ctx.enter_context(nc.semaphore(name))
            self.cnt[name] = 0
        self.q = {e: [] for e in self.ENGS}

    def op(self, eng, fn, wait=(), inc=(), nop=False):
        wmax = {}
        for s, v in wait:
            if v > 0:
                wmax[s] = max(v, wmax.get(s, 0))
        waits = tuple((self.sem[s], v) for s, v in wmax.items())
        ev = None
        incs = []
        for s, k in inc:
            self.cnt[s] += k
            incs.append((self.sem[s], k))
            ev = (s, self.cnt[s])

        def closure(e, fn=fn, waits=waits, incs=tuple(incs), nop=nop):
            for sem, v in waits:
                e.wait_ge(sem, v)
            if nop:
                e.engine_nop()
            ins = fn(e)
            for sem, k in incs:
                ins.then_inc(sem, k)

        self.q[eng].append(closure)
        return ev

    def wait(self, eng, *evs):
        wmax = {}
        for s, v in evs:
            if v > 0:
                wmax[s] = max(v, wmax.get(s, 0))
        waits = tuple((self.sem[s], v) for s, v in wmax.items())

        def closure(e, waits=waits):
            for sem, v in waits:
                e.wait_ge(sem, v)

        self.q[eng].append(closure)

    def flush(self):
        for e in self.ENGS:
            ops = self.q[e]
            if not ops:
                continue
            self.q[e] = []

            def body(engine, ops=ops):
                for f in ops:
                    f(engine)

            getattr(self.block, e)(body)

    def ev(self, s):
        return (s, self.cnt[s])


def build_program(tok, warm, debug=False):
    assert tok % TILE == 0 and warm % TILE == 0
    nt = tok // TILE
    out_tok = tok - warm

    nc = bass.Bass("TRN2", target_bir_lowering=False, dynamic_dma_scratch_size=2048)

    xT = nc.dram_tensor("xT", [128, tok // TILE, NCH, TILE], F32,
                        kind="ExternalInput").ap()
    w_in = {nm: nc.dram_tensor(nm, [D, D], F32, kind="ExternalInput").ap()
            for nm in ("wqT", "wkT", "wvT", "woT", "wrT", "wk2T", "wv2T")}
    nwv = nc.dram_tensor("nwv", [128, NCH], F32, kind="ExternalInput").ap()
    murv = nc.dram_tensor("murv", [128, NCH], F32, kind="ExternalInput").ap()
    mukv = nc.dram_tensor("mukv", [128, NCH], F32, kind="ExternalInput").ap()
    identv = nc.dram_tensor("identv", [128, 128], F32, kind="ExternalInput").ap()
    mask4v = nc.dram_tensor("mask4v", [128, 512], BF16, kind="ExternalInput").ap()
    onescv = nc.dram_tensor("onescv", [128, 1], F32, kind="ExternalInput").ap()
    onesrv = nc.dram_tensor("onesrv", [1, 128], F32, kind="ExternalInput").ap()

    outT = nc.dram_tensor("outT", [D, out_tok], F32, kind="ExternalOutput").ap()
    okind = dict(kind="ExternalOutput") if debug else {}
    o_scr = nc.dram_tensor("o_scr", [128, tok // TILE, NCH, TILE], F32, **okind).ap()
    rms_scr = nc.dram_tensor("rms_scr", [1, tok], F32, **okind).ap()
    x2_scr = nc.dram_tensor("x2_scr", [D, tok], F32, kind="ExternalOutput").ap() \
        if debug else None

    outTc = outT.rearrange("(c p) n -> p c n", p=128)

    ctx = ExitStack()
    bld = Bld(nc, ctx, ["LX", "LO", "LW", "LR", "SO0", "SO1", "SR", "SB", "SX",
                        "SW", "LB", "TT", "V", "S", "G", "T"])
    op, ev, C = bld.op, bld.ev, bld.cnt

    mask4 = ctx.enter_context(nc.sbuf_tensor("mask4", [128, 512], BF16))
    onesc = ctx.enter_context(nc.sbuf_tensor("onesc", [128, 1], F32))
    onescb = ctx.enter_context(nc.sbuf_tensor("onescb", [128, 1], BF16))
    onesr = ctx.enter_context(nc.sbuf_tensor("onesr", [1, 128], F32))
    nw = ctx.enter_context(nc.sbuf_tensor("nw", [128, NCH], F32))
    mur = ctx.enter_context(nc.sbuf_tensor("mur", [128, NCH], F32))
    muk = ctx.enter_context(nc.sbuf_tensor("muk", [128, NCH], F32))
    b_p44 = ctx.enter_context(nc.sbuf_tensor("b_p44", [128, 1], F32))
    b_m44 = ctx.enter_context(nc.sbuf_tensor("b_m44", [128, 1], F32))
    eps5 = ctx.enter_context(nc.sbuf_tensor("eps5", [1, 1], F32))
    eps6 = ctx.enter_context(nc.sbuf_tensor("eps6", [1, 1], F32))
    r_mean = ctx.enter_context(nc.sbuf_tensor("r_mean", [1, TILE], F32))
    r_msq = ctx.enter_context(nc.sbuf_tensor("r_msq", [1, TILE], F32))
    r_var = ctx.enter_context(nc.sbuf_tensor("r_var", [1, TILE], F32))
    r_a = ctx.enter_context(nc.sbuf_tensor("r_a", [1, TILE], F32))
    r_b = ctx.enter_context(nc.sbuf_tensor("r_b", [1, TILE], F32))
    r_rms = ctx.enter_context(nc.sbuf_tensor("r_rms", [1, TILE], F32))
    r_rr = r_rms
    r_t0 = r_msq
    wstage = ctx.enter_context(nc.sbuf_tensor("wstage", [128, 1, D], F32))

    p_pj = [ctx.enter_context(nc.psum_tensor(f"p_pj{i}", [128, 512], F32)) for i in range(2)]
    p_st = ctx.enter_context(nc.psum_tensor("p_st", [128, 512], F32))
    p_bc = ctx.enter_context(nc.psum_tensor("p_bc", [128, 512], F32))
    p_A = ctx.enter_context(nc.psum_tensor("p_A", [128, 512], F32))
    p_O = ctx.enter_context(nc.psum_tensor("p_O", [128, 512], F32))
    p_O2 = ctx.enter_context(nc.psum_tensor("p_O2", [128, 512], F32))
    p_A2 = ctx.enter_context(nc.psum_tensor("p_A2", [128, 512], F32))

    for dst, src in ((mask4, mask4v), (onesc, onescv), (onesr, onesrv),
                     (nw, nwv), (mur, murv), (muk, mukv)):
        op("sync", lambda e, dst=dst, src=src: e.dma_start(out=dst[:], in_=src),
           inc=[("LW", 16)])
    ev_cdma = ev("LW")
    op("vector", lambda e: e.memset(onescb[:], 1.0), inc=[("V", 1)], nop=True)
    op("vector", lambda e: e.memset(b_p44[:], MSH), inc=[("V", 1)], nop=True)
    op("vector", lambda e: e.memset(b_m44[:], -MSH), inc=[("V", 1)], nop=True)
    op("vector", lambda e: e.memset(eps5[:], EPS_LN), inc=[("V", 1)], nop=True)
    op("vector", lambda e: e.memset(eps6[:], EPS_RMS), inc=[("V", 1)], nop=True)

    def load_split_weight(src_ap, dst_h, dst_l, scale_nw=False, first_wait=(), eng="vector",
                          stage=None, lsem="LW"):
        E = "V" if eng == "vector" else "G"
        wst = wstage[:] if stage is None else stage
        srcc = src_ap.rearrange("(c p) n -> p c n", p=128)
        last = None
        for ci in range(NCH):
            wv = [(E, C[E])] + (list(first_wait) if ci == 0 else [])
            op("sync", lambda e, ci=ci, srcc=srcc, wst=wst: e.dma_start(
                out=wst, in_=srcc[:, ci:ci + 1, :]), wait=wv, inc=[(lsem, 16)])
            wl = [(lsem, C[lsem]), ev_cdma]
            if scale_nw:
                op(eng, lambda e, ci=ci, wst=wst: e.tensor_scalar_mul(
                    wst[:, 0, :], wst[:, 0, :], nw[:, ci:ci + 1]),
                   wait=wl + [(E, C[E])], inc=[(E, 1)])
                wl = [(E, C[E])]
            op(eng, lambda e, dst_h=dst_h, ci=ci, wst=wst: e.tensor_copy(
                dst_h[:, ci:ci + 1, :], wst), wait=wl + [(E, C[E])], inc=[(E, 1)])
            op(eng, lambda e, dst_h=dst_h, dst_l=dst_l, ci=ci, wst=wst: e.tensor_tensor(
                dst_l[:, ci:ci + 1, :], wst, dst_h[:, ci:ci + 1, :], ALU.subtract),
               wait=[(E, C[E])], inc=[(E, 1)])
            last = ev(E)
        return last

    state = {
        "bc_read": ("V", 0),
        "scr2_ld": ("G", 0),
        "bc_ev": ("T", 0),
        "pj_free": [("V", 0), ("V", 0)],
        "stats_free": ("V", 0),
    }

    def row_stats(eps_tile):
        op("vector", lambda e: e.tensor_scalar_mul(r_mean[:], p_st[0:1, 0:TILE], INV_D),
           wait=[("T", C["T"]), ("V", C["V"])], inc=[("V", 1)])
        op("vector", lambda e: e.tensor_tensor(r_msq[:], r_mean[:], r_mean[:], ALU.mult),
           wait=[("V", C["V"])], inc=[("V", 1)])
        op("vector", lambda e: e.scalar_tensor_tensor(
            r_var[:], p_st[32:33, 0:TILE], INV_D, r_msq[:], ALU.mult, ALU.subtract),
           wait=[("V", C["V"])], inc=[("V", 1)])
        op("scalar", lambda e: e.activation(out=r_t0[:], in_=r_var[:], func=AF.Sqrt,
                                            bias=eps_tile[:, 0:1]),
           wait=[("V", C["V"]), ("S", C["S"])], inc=[("S", 1)])
        op("vector", lambda e: e.reciprocal(r_a[:], r_t0[:]),
           wait=[("S", C["S"])], inc=[("V", 1)], nop=True)
        op("vector", lambda e: e.scalar_tensor_tensor(
            r_b[:], r_mean[:], -1.0, r_a[:], ALU.mult, ALU.mult),
           wait=[("V", C["V"])], inc=[("V", 1)])
        return ev("V")

    def bcast_rows():
        wv = [("V", C["V"]), state["bc_read"]]
        op("tensor", lambda e: e.matmul(p_bc[:, 0:TILE], onesr[:], r_a[:],
                                        start=True, stop=False), wait=wv)
        op("tensor", lambda e: e.matmul(p_bc[:, 256:256 + TILE], onesr[:], r_b[:],
                                        start=False, stop=True), inc=[("T", 1)])
        return ev("T")

    def bcA():
        return p_bc[:, 0:TILE].unsqueeze(1).broadcast_to([128, NCH, TILE])

    def bcB():
        return p_bc[:, 256:256 + TILE].unsqueeze(1).broadcast_to([128, NCH, TILE])

    def stats_reduce(x_ap, sq_ap, first_wait=()):
        wv = [state["stats_free"]] + list(first_wait)
        for ci in range(NCH):
            op("tensor", lambda e, ci=ci: e.matmul(
                p_st[0:1, 0:TILE], onesc[:], x_ap[:, ci, :],
                start=(ci == 0), stop=(ci == NCH - 1)),
               wait=wv if ci == 0 else ())
            op("tensor", lambda e, ci=ci: e.matmul(
                p_st[32:33, 0:TILE], onescb[:], sq_ap[:, ci, :],
                start=(ci == 0), stop=(ci == NCH - 1)),
               inc=[("T", 1)] if ci == NCH - 1 else ())
        return ev("T")

    def proj_feature(wh, wl, xh, xl, dj, extra_wait=()):
        bank = p_pj[dj % 2]
        wv = [state["pj_free"][dj % 2]] + list(extra_wait)
        first = True
        for ci in range(NCH):
            for (wop, xop) in ((wh, xh), (wh, xl), (wl, xh)):
                last = (ci == NCH - 1) and (wop is wl)
                op("tensor",
                   lambda e, wop=wop, xop=xop, ci=ci, dj=dj, first=first, last=last:
                   e.matmul(bank[:, 0:TILE], wop[:, ci, dj * 128:(dj + 1) * 128],
                            xop[:, ci, :], start=first, stop=last),
                   wait=wv if first else (), inc=[("T", 1)] if last else ())
                first = False
        return ev("T")

    ctxA = ExitStack()
    wA = {}
    for nm in ("wk", "wq", "wv"):
        wA[nm + "h"] = ctxA.enter_context(nc.sbuf_tensor(nm + "h", [128, NCH, D], BF16))
        wA[nm + "l"] = ctxA.enter_context(nc.sbuf_tensor(nm + "l", [128, NCH, D], BF16))
    x_sb = ctxA.enter_context(nc.sbuf_tensor("x_sb", [128, NCH, TILE], F32))
    qt_scr = ctxA.enter_context(nc.sbuf_tensor("qt_scr", [128, H, CH], F32))
    ln_h = ctxA.enter_context(nc.sbuf_tensor("ln_h", [128, NCH, TILE], BF16))
    ln_l = ctxA.enter_context(nc.sbuf_tensor("ln_l", [128, NCH, TILE], BF16))
    k_sb = ctxA.enter_context(nc.sbuf_tensor("k_sb", [128, H, TILE], F32))
    fq_sb = ctxA.enter_context(nc.sbuf_tensor("fq_sb", [128, 2, H, TILE], F32))
    g_sb = ctxA.enter_context(nc.sbuf_tensor("g_sb", [128, H, TILE], F32))
    xsq_bf = ctxA.enter_context(nc.sbuf_tensor("xsq_bf", [128, NCH, TILE], BF16))
    b_sb = ctxA.enter_context(nc.sbuf_tensor("b_sb", [128, H, CH], F32))
    ebq = ctxA.enter_context(nc.sbuf_tensor("ebq", [128, H, CH], F32))
    ebk = ctxA.enter_context(nc.sbuf_tensor("ebk", [128, H, CH], F32))
    qt_h = ctxA.enter_context(nc.sbuf_tensor("qt_h", [128, 2, H, CH], BF16))
    qt_l = ctxA.enter_context(nc.sbuf_tensor("qt_l", [128, 2, H, CH], BF16))
    kt_h = ctxA.enter_context(nc.sbuf_tensor("kt_h", [128, 2, H, CH], BF16))
    kt_l = ctxA.enter_context(nc.sbuf_tensor("kt_l", [128, 2, H, CH], BF16))
    A_h = ctxA.enter_context(nc.sbuf_tensor("A_h", [128, 512], BF16))
    A_l = ctxA.enter_context(nc.sbuf_tensor("A_l", [128, 512], BF16))
    A2_h = ctxA.enter_context(nc.sbuf_tensor("A2_h", [128, 512], BF16))
    A2_l = ctxA.enter_context(nc.sbuf_tensor("A2_l", [128, 512], BF16))
    osq_bf = xsq_bf[:][:, 0:4, 0:CH]
    o_stage = ctxA.enter_context(nc.sbuf_tensor("o_stage", [128, 4, CH], F32))
    vt_h = ctxA.enter_context(nc.sbuf_tensor("vt_h", [128, 2, H, HD], BF16))
    vt_l = ctxA.enter_context(nc.sbuf_tensor("vt_l", [128, 2, H, HD], BF16))
    kx_h = ctxA.enter_context(nc.sbuf_tensor("kx_h", [128, 3, H, CH], BF16))
    kx_l = ctxA.enter_context(nc.sbuf_tensor("kx_l", [128, 3, H, CH], BF16))
    Ax_h = ctxA.enter_context(nc.sbuf_tensor("Ax_h", [128, 2, 512], BF16))
    Ax_l = ctxA.enter_context(nc.sbuf_tensor("Ax_l", [128, 2, 512], BF16))

    ev_wk = load_split_weight(w_in["wkT"], wA["wkh"], wA["wkl"], eng="vector",
                              lsem="TT")
    ev_wq = load_split_weight(w_in["wqT"], wA["wqh"], wA["wql"], first_wait=[ev_wk],
                              eng="gpsimd", lsem="TT")
    ev_wv = load_split_weight(w_in["wvT"], wA["wvh"], wA["wvl"], first_wait=[ev_wq],
                              eng="vector", lsem="TT")
    bld.flush()

    st = {
        "x_free": ("V", 0),
        "fq_free": [("V", 0), ("V", 0)],
        "k_free": ("G", 0),
        "g_free": ("V", 0),
        "Abank_free": [("V", 0), ("V", 0)],
        "Obank_free": [[("SO0", 0), ("S", 0)], [("SO1", 0), ("S", 0)]],
        "ebq_free": ("V", 0),
        "ebx_done": ("S", 0),
        "ebk_free": ("G", 0),
        "osq_free": ("T", 0),
        "kt_free": [("T", 0), ("T", 0)],
        "vt_free": [("T", 0), ("T", 0)],
        "vt_ev": [("V", 0), ("V", 0)],
        "qt_free": [("T", 0), ("T", 0)],
        "kx_free": [("T", 0), ("T", 0), ("T", 0)],
        "kx_ready": [("G", 0), ("G", 0), ("G", 0)],
        "rms_free": [("V", 0), ("V", 0)],
        "stats_read": ("V", 0),
    }
    at = [dict(ev_g=("S", 0), ev_k=("G", 0), ev_q=("S", 0), ev_ln=("V", 0))
          for _ in range(nt + 1)]
    prep = {}
    n_ostore = 0
    n_rstore = [0]

    def a_prep(gc):
        q, cc = gc // 2, gc % 2
        tsl = slice(cc * CH, (cc + 1) * CH)
        par = gc % 2
        tpar = gc % 3
        fpar = q % 2
        aq = at[q]
        for h in range(H):
            wv = [aq["ev_g"], ("V", C["V"]), st["ebx_done"]] if h == 0 else ()
            op("vector", lambda e, h=h, tsl=tsl: e.tensor_tensor_scan(
                b_sb[:, h, :], g_sb[:, h, tsl], g_sb[:, h, tsl], 0.0, ALU.add, ALU.bypass),
               wait=wv, inc=[("V", 1)], nop=True)
        ev_b = ev("V")
        ev_ebq = op("scalar", lambda e: e.activation(out=ebq[:], in_=b_sb[:], func=AF.Exp,
                                                     bias=b_p44[:, 0:1]),
                    wait=[ev_b, st["ebq_free"], ("S", C["S"])], inc=[("S", 1)])
        ev_ebk = op("scalar", lambda e: e.activation(out=ebk[:], in_=b_sb[:], func=AF.Exp,
                                                     bias=b_m44[:, 0:1], scale=-1.0),
                    wait=[ev_b, st["ebk_free"]], inc=[("S", 1)])
        op("vector", lambda e, tsl=tsl, fpar=fpar: e.tensor_tensor(
            qt_scr[:], fq_sb[:, fpar, :, tsl], ebq[:], ALU.mult),
           wait=[aq["ev_q"], ev_ebq, ("V", C["V"])], inc=[("V", 1)])
        ev_qtmul = ev("V")
        op("vector", lambda e, par=par: e.tensor_copy(qt_h[:, par], qt_scr[:]),
           wait=[st["qt_free"][par], ("V", C["V"])], inc=[("V", 1)])
        op("vector", lambda e, par=par: e.tensor_tensor(qt_l[:, par], qt_scr[:],
                                                        qt_h[:, par], ALU.subtract),
           wait=[("V", C["V"])], inc=[("V", 1)])
        ev_qt = ev("V")
        op("gpsimd", lambda e, tsl=tsl: e.tensor_tensor(ebk[:], k_sb[:, :, tsl], ebk[:],
                                                        ALU.mult),
           wait=[aq["ev_k"], ev_ebk, ("G", C["G"])], inc=[("G", 1)])
        ev_ktmul = ev("G")
        ev_kth = op("scalar", lambda e, par=par: e.activation(
            out=kt_h[:, par], in_=ebk[:], func=AF.Copy),
            wait=[st["kt_free"][par], ev_ktmul, ("S", C["S"])], inc=[("S", 1)])
        op("gpsimd", lambda e, par=par: e.tensor_tensor(kt_l[:, par], ebk[:], kt_h[:, par],
                                                        ALU.subtract),
           wait=[ev_kth, ("G", C["G"])], inc=[("G", 1)])
        ev_kt = ev("G")
        for h in range(H):
            op("scalar", lambda e, h=h: e.activation(
                out=ebq[:, h, :], in_=b_sb[:, h, :], func=AF.Exp, scale=-1.0,
                bias=b_sb[:, h, CH - 1:CH]),
               wait=[ev_qtmul, ev_b, ("S", C["S"])], inc=[("S", 1)])
        ev_ebx = ev("S")
        st["ebx_done"] = ev_ebx
        ev_kxmul = op("gpsimd", lambda e, tsl=tsl: e.tensor_tensor(
            ebq[:], k_sb[:, :, tsl], ebq[:], ALU.mult),
           wait=[ev_ebx, ("G", C["G"])], inc=[("G", 1)])
        ev_kxh = op("scalar", lambda e, tpar=tpar: e.activation(
            out=kx_h[:, tpar], in_=ebq[:], func=AF.Copy),
            wait=[st["kx_free"][tpar], ev_kxmul, ("S", C["S"])], inc=[("S", 1)])
        op("gpsimd", lambda e, tpar=tpar: e.tensor_tensor(kx_l[:, tpar], ebq[:],
                                                          kx_h[:, tpar], ALU.subtract),
           wait=[ev_kxh, ("G", C["G"])], inc=[("G", 1)])
        ev_kx = ev("G")
        st["kx_ready"][tpar] = ev_kx
        st["ebq_free"] = ev_kx
        st["ebk_free"] = ev_kt
        st["k_free"] = ("G", C["G"])
        if cc == 1:
            st["fq_free"][fpar] = ev_qtmul
            st["g_free"] = ev_b
        prep.setdefault(gc, {}).update(
            dict(ev_b=ev_b, ev_qt=ev_qt, ev_kt=ev_kt, par=par, tpar=tpar))

    def a_vproj(gc):
        q, cc = gc // 2, gc % 2
        tsl = slice(cc * CH, (cc + 1) * CH)
        par = gc % 2
        for half in range(2):
            bank = p_pj[half]
            first = True
            for ci in range(NCH):
                for (wop, lnop) in ((wA["wvh"], ln_h), (wA["wvh"], ln_l), (wA["wvl"], ln_h)):
                    last = (ci == NCH - 1) and (wop is wA["wvl"])
                    op("tensor",
                       lambda e, wop=wop, lnop=lnop, ci=ci, half=half, first=first,
                       last=last, tsl=tsl, bank=bank:
                       e.matmul(bank[:], lnop[:, ci, tsl],
                                wop[:, ci, half * 512:(half + 1) * 512],
                                start=first, stop=last),
                       wait=[state["pj_free"][half], st["vt_free"][par], ev_wv]
                       if first else (),
                       inc=[("T", 1)] if last else ())
                    first = False
        ev_vmm = ev("T")
        op("vector", lambda e, par=par: e.tensor_copy(vt_h[:, par, 0:4, :], p_pj[0][:]),
           wait=[ev_vmm, ("V", C["V"])], inc=[("V", 1)])
        op("vector", lambda e, par=par: e.tensor_copy(vt_h[:, par, 4:8, :], p_pj[1][:]),
           wait=[("V", C["V"])], inc=[("V", 1)])
        op("vector", lambda e, par=par: e.tensor_tensor(
            vt_l[:, par, 0:4, :], p_pj[0][:], vt_h[:, par, 0:4, :], ALU.subtract),
           wait=[("V", C["V"])], inc=[("V", 1)])
        op("vector", lambda e, par=par: e.tensor_tensor(
            vt_l[:, par, 4:8, :], p_pj[1][:], vt_h[:, par, 4:8, :], ALU.subtract),
           wait=[("V", C["V"])], inc=[("V", 1)])
        ev_vev = ev("V")
        st["vt_ev"][par] = ev_vev
        state["pj_free"][0] = ev_vev
        state["pj_free"][1] = ev_vev
        prep.setdefault(gc, {})["ev_vev"] = ev_vev

    def a_gla(gc):
        q, cc = gc // 2, gc % 2
        tsl = slice(cc * CH, (cc + 1) * CH)
        p_ = prep[gc]
        par = p_["par"]
        xpar = (p_["tpar"] + 2) % 3
        first_chunk = (gc == 0)
        ev_vev = p_["ev_vev"]
        qh, ql = qt_h[:, par], qt_l[:, par]
        kh, kl = kt_h[:, par], kt_l[:, par]
        ev_Amm = [None, None]
        for g4 in range(2):
            hs = [4 * g4 + i for i in range(4)]
            bank = p_A if g4 == 0 else p_A2
            bfree = st["Abank_free"][g4]
            first = True
            for h in hs:
                for (kop, qop) in ((kh, qh), (kh, ql), (kl, qh)):
                    last = (h == hs[-1]) and (kop is kl)
                    op("tensor",
                       lambda e, kop=kop, qop=qop, h=h, first=first, last=last,
                       g4=g4, bank=bank:
                       e.matmul(bank[:, (h - 4 * g4) * 128:(h - 4 * g4 + 1) * 128],
                                kop[:, h, :], qop[:, h, :], start=first, stop=last),
                       wait=[p_["ev_kt"], p_["ev_qt"], bfree] if first else (),
                       inc=[("T", 1)] if last else ())
                    first = False
            ev_Amm[g4] = ev("T")
        ev_Aev = [None, None]
        for g4 in range(2):
            bank = p_A if g4 == 0 else p_A2
            Ah = A_h if g4 == 0 else A2_h
            Al = A_l if g4 == 0 else A2_l
            op("vector", lambda e, bank=bank: e.tensor_tensor(
                bank[:], bank[:], mask4[:], ALU.mult),
               wait=[ev_Amm[g4], ("V", C["V"])], inc=[("V", 1)])
            op("vector", lambda e, bank=bank, Ah=Ah: e.tensor_copy(Ah[:], bank[:]),
               wait=[("V", C["V"])], inc=[("V", 1)])
            op("vector", lambda e, bank=bank, Ah=Ah, Al=Al: e.tensor_tensor(
                Al[:], bank[:], Ah[:], ALU.subtract),
               wait=[("V", C["V"])], inc=[("V", 1)])
            ev_Aev[g4] = ev("V")
            st["Abank_free"][g4] = ev_Aev[g4]
        if not first_chunk:
            ev_Axmm = [None, None]
            for g4 in range(2):
                hs = [4 * g4 + i for i in range(4)]
                bank = p_A if g4 == 0 else p_A2
                first = True
                for h in hs:
                    for (kop, qop) in ((kx_h, qh), (kx_h, ql), (kx_l, qh)):
                        last = (h == hs[-1]) and (kop is kx_l)
                        op("tensor",
                           lambda e, kop=kop, qop=qop, h=h, first=first, last=last,
                           g4=g4, bank=bank, xpar=xpar:
                           e.matmul(bank[:, (h - 4 * g4) * 128:(h - 4 * g4 + 1) * 128],
                                    kop[:, xpar, h, :], qop[:, h, :],
                                    start=first, stop=last),
                           wait=[st["kx_ready"][xpar], p_["ev_qt"], ev_Aev[g4]]
                           if first else (),
                           inc=[("T", 1)] if last else ())
                        first = False
                ev_Axmm[g4] = ev("T")
            st["kx_free"][xpar] = ev_Axmm[1]
            ev_Axev = [None, None]
            for g4 in range(2):
                bank = p_A if g4 == 0 else p_A2
                op("vector", lambda e, bank=bank, g4=g4: e.tensor_scalar_mul(
                    Ax_h[:, g4, :], bank[:], E_MSH),
                   wait=[ev_Axmm[g4], ("V", C["V"])], inc=[("V", 1)])
                op("vector", lambda e, bank=bank, g4=g4: e.scalar_tensor_tensor(
                    Ax_l[:, g4, :], bank[:], E_MSH, Ax_h[:, g4, :], ALU.mult,
                    ALU.subtract),
                   wait=[("V", C["V"])], inc=[("V", 1)])
                ev_Axev[g4] = ev("V")
                st["Abank_free"][g4] = ev_Axev[g4]

        ev_rms_mm = None
        for g4 in range(2):
            hs = [4 * g4 + i for i in range(4)]
            Ah = A_h if g4 == 0 else A2_h
            Al = A_l if g4 == 0 else A2_l
            obank = p_O if g4 == 0 else p_O2
            first = True
            for hi_, h in enumerate(hs):
                for pi, (vop, aop) in enumerate(((vt_h, Ah), (vt_h, Al), (vt_l, Ah))):
                    last = first_chunk and (hi_ == 3) and (pi == 2)
                    op("tensor",
                       lambda e, vop=vop, aop=aop, h=h, g4=g4, first=first, last=last,
                       obank=obank, par=par:
                       e.matmul(obank[:, (h - 4 * g4) * 128:(h - 4 * g4 + 1) * 128],
                                vop[:, par, h, :],
                                aop[:, (h - 4 * g4) * 128:(h - 4 * g4 + 1) * 128],
                                start=first, stop=last),
                       wait=[ev_Aev[g4], ev_vev] + st["Obank_free"][g4]
                       if first else (),
                       inc=[("T", 1)] if last else ())
                    first = False
            if not first_chunk:
                ppar = 1 - par
                for hi_, h in enumerate(hs):
                    for pi, (vop, axop) in enumerate(((vt_h, Ax_h), (vt_h, Ax_l),
                                                      (vt_l, Ax_h))):
                        last = (hi_ == 3) and (pi == 2)
                        op("tensor",
                           lambda e, vop=vop, axop=axop, h=h, g4=g4, last=last,
                           obank=obank, ppar=ppar:
                           e.matmul(obank[:, (h - 4 * g4) * 128:(h - 4 * g4 + 1) * 128],
                                    vop[:, ppar, h, :],
                                    axop[:, g4, (h - 4 * g4) * 128:(h - 4 * g4 + 1) * 128],
                                    start=False, stop=last),
                           wait=[ev_Axev[g4], st["vt_ev"][ppar]]
                           if (hi_ == 0 and pi == 0) else (),
                           inc=[("T", 1)] if last else ())
            ev_Omm = ev("T")
            if g4 == 1:
                st["qt_free"][par] = ev_Omm
                st["kt_free"][par] = ev_Amm[1]
                if not first_chunk:
                    st["vt_free"][1 - par] = ev_Omm
                else:
                    st["vt_free"][1 - par] = ("T", C["T"])
            so = "SO0" if g4 == 0 else "SO1"
            ev_oev = op("scalar", lambda e, obank=obank: e.activation(
                out=o_stage[:], in_=obank[:].rearrange("p (a c) -> p a c", c=128),
                func=AF.Copy),
                wait=[ev_Omm, ("SO0", C["SO0"]), ("SO1", C["SO1"]), ("S", C["S"])],
                inc=[("S", 1)])
            ev_osq = op("scalar", lambda e: e.activation(
                out=osq_bf, in_=o_stage[:], func=AF.Square),
                wait=[ev_oev, st["osq_free"], ("S", C["S"])], inc=[("S", 1)])
            for i in range(4):
                op("tensor", lambda e, i=i, cc=cc, g4=g4: e.matmul(
                    p_st[64:65, 256 + cc * CH:256 + (cc + 1) * CH], onescb[:],
                    osq_bf[:, i, :], start=(g4 == 0 and i == 0),
                    stop=(g4 == 1 and i == 3)),
                   wait=[ev_osq, st["rms_free"][cc]] if i == 0 else (),
                   inc=[("T", 1)] if (g4 == 1 and i == 3) else ())
            st["osq_free"] = ("T", C["T"])
            ev_rms_mm = ("T", C["T"])
            op("sync", lambda e, g4=g4, q=q, cc=cc: e.dma_start(
                out=o_scr[:, q, 4 * g4:4 * g4 + 4, cc * CH:(cc + 1) * CH],
                in_=o_stage[:]),
               wait=[ev_oev], inc=[(so, 16)])
            st["Obank_free"][g4] = [("S", C["S"])]

        op("vector", lambda e, cc=cc: e.tensor_scalar_mul(
            r_rms[:, cc * CH:(cc + 1) * CH],
            p_st[64:65, 256 + cc * CH:256 + (cc + 1) * CH], INV_D),
           wait=[ev_rms_mm, ("SR", n_rstore[0] * 16), ("V", C["V"])], inc=[("V", 1)])
        op("scalar", lambda e, cc=cc: e.activation(
            out=r_rms[:, cc * CH:(cc + 1) * CH], in_=r_rms[:, cc * CH:(cc + 1) * CH],
            func=AF.Sqrt, bias=eps6[:, 0:1]),
           wait=[("V", C["V"]), ("S", C["S"])], inc=[("S", 1)])
        op("vector", lambda e, cc=cc: e.reciprocal(
            r_rms[:, cc * CH:(cc + 1) * CH], r_rms[:, cc * CH:(cc + 1) * CH]),
           wait=[("S", C["S"])], inc=[("V", 1)], nop=True)
        st["rms_free"][cc] = ev("V")
        if cc == 1:
            op("sync", lambda e, q=q: e.dma_start(
                out=rms_scr[0:1, q * TILE:q * TILE + TILE], in_=r_rms[:]),
               wait=[st["rms_free"][0], st["rms_free"][1]], inc=[("SR", 16)])
            n_rstore[0] += 1

    def a_stats(p):
        ts0 = p * TILE
        op("sync", lambda e, p=p: e.dma_start(out=x_sb[:], in_=xT[:, p]),
           wait=[st["x_free"]], inc=[("LX", 16)])
        ev_x = ev("LX")
        ev_sq = op("scalar", lambda e: e.activation(out=xsq_bf[:], in_=x_sb[:], func=AF.Square),
                   wait=[ev_x, ("T", C["T"]), state["stats_free"], ("S", C["S"])],
                   inc=[("S", 1)])
        stats_reduce(x_sb, xsq_bf[:], first_wait=[ev_x, ev_sq])
        row_stats(eps5)
        state["stats_free"] = ev("V")
        at[p]["ev_x"] = ev_x

    def a_bc(p):
        state["bc_ev"] = bcast_rows()

    def a_ln(p):
        op("vector", lambda e: e.tensor_tensor(x_sb[:], x_sb[:], bcA(), ALU.mult),
           wait=[state["bc_ev"], ("V", C["V"])], inc=[("V", 1)])
        op("vector", lambda e: e.tensor_tensor(x_sb[:], x_sb[:], bcB(), ALU.add),
           wait=[("V", C["V"])], inc=[("V", 1)])
        state["bc_read"] = ev("V")
        op("vector", lambda e: e.tensor_copy(ln_h[:], x_sb[:]),
           wait=[("V", C["V"])], inc=[("V", 1)])
        op("vector", lambda e: e.tensor_tensor(ln_l[:], x_sb[:], ln_h[:], ALU.subtract),
           wait=[("V", C["V"])], inc=[("V", 1)])
        ev_ln = ev("V")
        at[p]["ev_ln"] = ev_ln
        st["x_free"] = ev_ln

    def a_wk(p):
        fpar = p % 2
        ev_fev = None
        for dj in range(NCH):
            evt = proj_feature(wA["wkh"], wA["wkl"], ln_h, ln_l, dj,
                               extra_wait=[at[p]["ev_ln"], st["fq_free"][fpar], ev_wk]
                               if dj < 2 else [])
            ev_fev = op("scalar", lambda e, dj=dj, fpar=fpar: e.activation(
                out=fq_sb[:, fpar, dj, :], in_=p_pj[dj % 2][:, 0:TILE], func=AF.Sigmoid),
                wait=[evt], inc=[("S", 1)])
            state["pj_free"][dj % 2] = ev_fev
        wv = [ev_fev, st["g_free"], ("S", C["S"])]
        op("scalar", lambda e, fpar=fpar: e.activation(out=g_sb[:], in_=fq_sb[:, fpar],
                                                       func=AF.Sigmoid),
           wait=wv, inc=[("S", 1)])
        op("scalar", lambda e: e.activation(out=g_sb[:], in_=g_sb[:], func=AF.Ln),
           wait=[("S", C["S"])], inc=[("S", 1)])
        at[p]["ev_g"] = ev("S")
        at[p]["ev_k"] = op("gpsimd", lambda e, fpar=fpar: e.tensor_scalar(
            out=k_sb[:], in0=fq_sb[:, fpar], scalar1=-1.0, scalar2=1.0,
            op0=ALU.mult, op1=ALU.add),
            wait=[ev_fev, st["k_free"]], inc=[("G", 1)])

    def a_wq(p):
        fpar = p % 2
        ev_qev = None
        for dj in range(NCH):
            evt = proj_feature(wA["wqh"], wA["wql"], ln_h, ln_l, dj,
                               extra_wait=[ev_wq] if dj < 2 else [])
            ev_qev = op("scalar", lambda e, dj=dj, fpar=fpar: e.activation(
                out=fq_sb[:, fpar, dj, :], in_=p_pj[dj % 2][:, 0:TILE], func=AF.Silu),
                wait=[evt, at[p]["ev_g"], at[p]["ev_k"]] if dj == 0 else [evt],
                inc=[("S", 1)])
            state["pj_free"][dj % 2] = ev_qev
        at[p]["ev_q"] = ev_qev
        st["ln_free"] = ("T", C["T"])

    a_stats(0)
    for it in range(nt + 1):
        p, q = it, it - 1
        if p < nt:
            a_bc(p)
        if q >= 0:
            a_prep(2 * q)
            a_prep(2 * q + 1)
            a_vproj(2 * q)
            a_gla(2 * q)
            a_vproj(2 * q + 1)
        if p < nt:
            a_ln(p)
            a_wk(p)
        if q >= 0:
            a_gla(2 * q + 1)
        if p < nt:
            a_wq(p)
            if p + 1 < nt:
                a_stats(p + 1)
        bld.flush()

    n_rstore = n_rstore[0]
    ev_A_drain = [(s, C[s]) for s in ("V", "S", "G", "T")] + \
                 [("SO0", C["SO0"]), ("SO1", C["SO1"]), ("SR", n_rstore * 16),
                  ("LX", C["LX"])]
    bld.flush()
    ctxA.close()


    ctxB = ExitStack()
    wB = {}
    for nm in ("wo", "wr", "wk2", "wv2"):
        wB[nm + "h"] = ctxB.enter_context(nc.sbuf_tensor(nm + "h", [128, NCH, D], BF16))
        wB[nm + "l"] = ctxB.enter_context(nc.sbuf_tensor(nm + "l", [128, NCH, D], BF16))
    xB = ctxB.enter_context(nc.sbuf_tensor("xB", [128, NCH, TILE], F32))
    x2_sb = ctxB.enter_context(nc.sbuf_tensor("x2_sb", [128, NCH, TILE], F32))
    oB = ctxB.enter_context(nc.sbuf_tensor("oB", [128, NCH, TILE], F32))
    oB_bf = oB[:].bitcast(BF16)
    sqA = oB_bf[:, :, 0:TILE]
    sqB = oB_bf[:, :, TILE:2 * TILE]
    o_pair = ctxB.enter_context(nc.sbuf_tensor("o_pair", [128, NCH, 2, TILE], BF16))
    o_h = o_pair[:][:, :, 0, :]
    o_l = o_pair[:][:, :, 1, :]

    r_sb = ctxB.enter_context(nc.sbuf_tensor("r_sb", [128, NCH, TILE], F32))
    ffv = r_sb[:]
    ln2 = ctxB.enter_context(nc.sbuf_tensor("ln2", [128, NCH, TILE + 1], F32))
    scr1 = ctxB.enter_context(nc.sbuf_tensor("scr1", [128, NCH, TILE], F32))
    scr2 = ctxB.enter_context(nc.sbuf_tensor("scr2", [128, NCH, TILE], F32))
    scr2_bf = scr2[:].bitcast(BF16)[:, :, 0:TILE]
    xr_h = ctxB.enter_context(nc.sbuf_tensor("xr_h", [128, NCH, TILE], BF16))
    xr_l = ctxB.enter_context(nc.sbuf_tensor("xr_l", [128, NCH, TILE], BF16))
    xk_h = ctxB.enter_context(nc.sbuf_tensor("xk_h", [128, NCH, TILE], BF16))
    xk_l = ctxB.enter_context(nc.sbuf_tensor("xk_l", [128, NCH, TILE], BF16))
    kc_h = xk_h
    kc_l = xk_l


    TZ, VZ, SZ = ("T", 0), ("V", 0), ("S", 0)
    bt = [dict(xk=VZ, x2=VZ, x3=VZ, WoT=TZ, WrT=TZ, Wv2T=TZ, ff=VZ, kc=VZ, aoRM=VZ,
               ffstat=TZ, aostat=TZ, sqB_T=TZ, revac=SZ, ffsq=SZ) for _ in range(nt + 1)]
    state["pj_free"] = [("V", 0), ("V", 0)]
    n_bstore = [0]

    def b_wr(q):
        me = bt[q]
        ev_rev = None
        for dj in range(NCH):
            evt = proj_feature(wB["wrh"], wB["wrl"], xr_h, xr_l, dj,
                               extra_wait=[me["xk"], ev_wr] if dj < 2 else [])
            ev_rev = op("scalar", lambda e, dj=dj: e.activation(
                out=r_sb[:, dj, :], in_=p_pj[dj % 2][:, 0:TILE], func=AF.Sigmoid),
                wait=[evt, bt[q - 1]["x3"] if q > 0 else VZ,
                      bt[q - 1]["ffstat"] if q > 0 else TZ, ("S", C["S"])]
                if dj == 0 else [evt],
                inc=[("S", 1)])
            state["pj_free"][dj % 2] = ev_rev
        me["WrT"] = ("T", C["T"])
        me["revac"] = ev("S")
        me["r"] = ev_rev

    def b_load_dma(p):
        ts0 = p * TILE
        pv = bt[p - 1] if p > 0 else bt[nt]
        op("sync", lambda e, p=p: e.dma_start(out=oB[:], in_=o_scr[:, p]),
           wait=[("SO0", C["SO0"]), ("SO1", C["SO1"]), pv["sqB_T"], pv["aostat"]],
           inc=[("LO", 16)])
        op("sync", lambda e, ts0=ts0: e.dma_start(out=r_rr[:], in_=rms_scr[0:1, ts0:ts0 + TILE]),
           wait=[("SR", n_rstore * 16), pv["aoRM"]], inc=[("LR", 16)])
        if p == 0:
            op("sync", lambda e: e.dma_start(out=xB[:], in_=xT[:, 0]),
               wait=[], inc=[("LX", 16)])

    def b_load_wo(p):
        pv = bt[p - 1] if p > 0 else bt[nt]
        me = bt[p]
        if p > 0:
            b_load_dma(p)
        ev_o = ev("LO")
        ev_rrow = ev("LR")
        me["x_ld"] = ("LX", C["LX"])
        op("gpsimd", lambda e: e.tensor_copy(o_h, oB[:]),
           wait=[ev_o, pv["WoT"], pv["revac"], ("G", C["G"])], inc=[("G", 1)])
        op("gpsimd", lambda e: e.tensor_tensor(o_l, oB[:], o_h, ALU.subtract),
           wait=[("G", C["G"])], inc=[("G", 1)])
        ev_ospl = ev("G")
        ev_aoev = None
        for dj in range(NCH):
            evt = proj_feature(wB["woh"], wB["wol"], o_h, o_l, dj,
                               extra_wait=[ev_ospl, ev_wo] if dj < 2 else [])
            ev_aoev = op("scalar", lambda e, dj=dj: e.activation(
                out=scr1[:, dj, :], in_=p_pj[dj % 2][:, 0:TILE], func=AF.Copy),
                wait=[evt, pv["xk"], ("S", C["S"])], inc=[("S", 1)])
            state["pj_free"][dj % 2] = ev_aoev
        me["WoT"] = ("T", C["T"])
        ev_sq = op("scalar", lambda e: e.activation(out=sqA, in_=scr1[:], func=AF.Square),
                   wait=[ev_aoev, ("LO", C["LO"]), state["stats_free"], ("S", C["S"])],
                   inc=[("S", 1)])
        stats_reduce(scr1, sqA, first_wait=[ev_aoev, ev_sq])
        me["aostat"] = ("T", C["T"])
        op("vector", lambda e: e.tensor_scalar_mul(r_mean[:], p_st[0:1, 0:TILE], INV_D),
           wait=[("T", C["T"]), ("V", C["V"])], inc=[("V", 1)])
        op("vector", lambda e: e.tensor_tensor(r_msq[:], r_mean[:], r_mean[:], ALU.mult),
           wait=[("V", C["V"])], inc=[("V", 1)])
        op("vector", lambda e: e.scalar_tensor_tensor(
            r_var[:], p_st[32:33, 0:TILE], INV_D, r_msq[:], ALU.mult, ALU.subtract),
           wait=[("V", C["V"])], inc=[("V", 1)])
        op("vector", lambda e: e.tensor_tensor(r_t0[:], r_rr[:], r_rr[:], ALU.mult),
           wait=[ev_rrow, ("V", C["V"])], inc=[("V", 1)])
        op("vector", lambda e: e.tensor_tensor(r_var[:], r_var[:], r_t0[:], ALU.mult),
           wait=[("V", C["V"])], inc=[("V", 1)])
        op("scalar", lambda e: e.activation(out=r_t0[:], in_=r_var[:], func=AF.Sqrt,
                                            bias=eps5[:, 0:1]),
           wait=[("V", C["V"]), ("S", C["S"])], inc=[("S", 1)])
        op("vector", lambda e: e.reciprocal(r_var[:], r_t0[:]),
           wait=[("S", C["S"])], inc=[("V", 1)], nop=True)
        op("vector", lambda e: e.tensor_tensor(r_a[:], r_var[:], r_rr[:], ALU.mult),
           wait=[("V", C["V"])], inc=[("V", 1)])
        op("vector", lambda e: e.scalar_tensor_tensor(
            r_b[:], r_mean[:], -1.0, r_a[:], ALU.mult, ALU.mult),
           wait=[("V", C["V"])], inc=[("V", 1)])
        state["stats_free"] = ev("V")
        me["aoRM"] = ev("V")

    def b_wk2(q):
        me = bt[q]
        ev_kev = None
        for dj in range(NCH):
            evt = proj_feature(wB["wk2h"], wB["wk2l"], xk_h, xk_l, dj,
                               extra_wait=[ev_wk2, me["xk"]] if dj < 2 else [])
            ev_kev = op("scalar", lambda e, dj=dj: e.activation(
                out=scr2[:, dj, :], in_=p_pj[dj % 2][:, 0:TILE], func=AF.Relu),
                wait=[evt, bt[q - 1]["ffstat"] if q > 0 else state["scr2_ld"],
                      ("S", C["S"])]
                if dj == 0 else [evt],
                inc=[("S", 1)])
            state["pj_free"][dj % 2] = ev_kev
        ev_wk2T = ("T", C["T"])
        op("vector", lambda e: e.tensor_tensor(scr2[:], scr2[:], scr2[:], ALU.mult),
           wait=[ev_kev, ("V", C["V"])], inc=[("V", 1)])
        op("vector", lambda e: e.tensor_copy(kc_h[:], scr2[:]),
           wait=[("V", C["V"]), ev_wk2T], inc=[("V", 1)])
        op("vector", lambda e: e.tensor_tensor(kc_l[:], scr2[:], kc_h[:], ALU.subtract),
           wait=[("V", C["V"])], inc=[("V", 1)])
        me["kc"] = ev("V")

    def b_wv2(q):
        me = bt[q]
        ev_ffev = None
        for dj in range(NCH):
            evt = proj_feature(wB["wv2h"], wB["wv2l"], kc_h, kc_l, dj,
                               extra_wait=[me["kc"], ev_wv2] if dj < 2 else [])
            ev_ffev = op("vector", lambda e, dj=dj: e.tensor_tensor(
                ffv[:, dj, :], p_pj[dj % 2][:, 0:TILE], r_sb[:, dj, :], ALU.mult),
                wait=[evt, me["r"], bt[q - 1]["ffsq"] if q > 0 else SZ,
                      bt[q - 1]["ffstat"] if q > 0 else TZ, ("V", C["V"])],
                inc=[("V", 1)])
            state["pj_free"][dj % 2] = ev_ffev
        me["Wv2T"] = ("T", C["T"])
        me["ff"] = ev_ffev

    def b_aobc(p):
        me = bt[p]
        bcast_rows()
        me["aobc"] = ("T", C["T"])
        op("vector", lambda e: e.tensor_tensor(scr1[:], scr1[:], bcA(), ALU.mult),
           wait=[me["aobc"], ("V", C["V"])], inc=[("V", 1)])
        op("vector", lambda e: e.tensor_tensor(scr1[:], scr1[:], bcB(), ALU.add),
           wait=[("V", C["V"])], inc=[("V", 1)])
        state["bc_read"] = ev("V")

    def b_ffstat(q):
        me = bt[q]
        ev_sq = op("scalar", lambda e: e.activation(out=scr2_bf, in_=ffv, func=AF.Square),
                   wait=[me["ff"], me["kc"], state["stats_free"], ("S", C["S"])],
                   inc=[("S", 1)])
        me["ffsq"] = ev_sq
        stats_reduce(ffv, scr2_bf, first_wait=[me["ff"], ev_sq])
        me["ffstat"] = ("T", C["T"])
        row_stats(eps5)
        state["stats_free"] = ev("V")

    def b_fin(q):
        ts0 = q * TILE
        me = bt[q]
        bcast_rows()
        op("vector", lambda e: e.tensor_tensor(ffv, ffv, bcA(), ALU.mult),
           wait=[("T", C["T"]), ("V", C["V"])], inc=[("V", 1)])
        op("vector", lambda e: e.tensor_tensor(ffv, ffv, bcB(), ALU.add),
           wait=[("V", C["V"])], inc=[("V", 1)])
        state["bc_read"] = ev("V")
        op("vector", lambda e: e.tensor_tensor(x2_sb[:], x2_sb[:], ffv, ALU.add),
           wait=[("V", C["V"]), ("SX", C["SX"])], inc=[("V", 1)])
        me["x3"] = ev("V")
        if ts0 >= warm:
            op("sync", lambda e, ts0=ts0: e.dma_start(
                out=outTc[:, :, ts0 - warm:ts0 - warm + TILE], in_=x2_sb[:]),
               wait=[me["x3"]], inc=[("SB", 16)])
            n_bstore[0] += 1

    def b_x2(p):
        ts0 = p * TILE
        me = bt[p]
        op("vector", lambda e: e.tensor_tensor(x2_sb[:], xB[:], scr1[:], ALU.add),
           wait=[me["x_ld"], ("SB", n_bstore[0] * 16), ("V", C["V"])], inc=[("V", 1)])
        me["x2"] = ev("V")
        if debug:
            op("sync", lambda e, ts0=ts0: e.dma_start(
                out=x2_scr.rearrange("(c p) n -> p c n", p=128)[:, :, ts0:ts0 + TILE],
                in_=x2_sb[:]), wait=[me["x2"]], inc=[("SX", 16)])
        if p + 1 < nt:
            op("sync", lambda e, p=p: e.dma_start(out=xB[:], in_=xT[:, p + 1]),
               wait=[me["x2"]], inc=[("LX", 16)])

    def b_ln2_xrxk(p):
        me = bt[p]
        pv = bt[p - 1] if p > 0 else bt[nt]
        ev_sq = op("scalar", lambda e: e.activation(out=sqB, in_=x2_sb[:], func=AF.Square),
                   wait=[me["x2"], ("LO", C["LO"]), ("S", C["S"]), state["stats_free"]],
                   inc=[("S", 1)])
        stats_reduce(x2_sb, sqB, first_wait=[me["x2"], ev_sq])
        me["sqB_T"] = ("T", C["T"])
        row_stats(eps5)
        state["stats_free"] = ev("V")
        bcast_rows()
        op("vector", lambda e: e.tensor_copy(ln2[:, :, 0:1], ln2[:, :, TILE:TILE + 1]),
           wait=[("V", C["V"]), ev_ln2init], inc=[("V", 1)])
        op("vector", lambda e: e.tensor_tensor(ln2[:, :, 1:TILE + 1], x2_sb[:], bcA(),
                                               ALU.mult),
           wait=[("T", C["T"]), ("V", C["V"])], inc=[("V", 1)])
        op("vector", lambda e: e.tensor_tensor(ln2[:, :, 1:TILE + 1], ln2[:, :, 1:TILE + 1],
                                               bcB(), ALU.add),
           wait=[("V", C["V"])], inc=[("V", 1)])
        state["bc_read"] = ev("V")
        ev_ln2 = ev("V")
        op("vector", lambda e: e.tensor_tensor(scr1[:], ln2[:, :, 1:TILE + 1],
                                               ln2[:, :, 0:TILE], ALU.subtract),
           wait=[ev_ln2], inc=[("V", 1)])
        mur_bc = mur[:].unsqueeze(2).broadcast_to([128, NCH, TILE])
        muk_bc = muk[:].unsqueeze(2).broadcast_to([128, NCH, TILE])
        q = p - 1
        gate_scr2 = bt[q]["ffstat"] if q >= 0 else state["scr2_ld"]
        gate_xr = bt[q]["WrT"] if q >= 0 else TZ
        gate_xk = bt[q]["Wv2T"] if q >= 0 else TZ
        op("vector", lambda e: e.tensor_tensor(scr2[:], scr1[:], mur_bc, ALU.mult),
           wait=[("V", C["V"]), gate_scr2, ("S", C["S"])], inc=[("V", 1)])
        op("vector", lambda e: e.tensor_tensor(scr2[:], scr2[:], ln2[:, :, 0:TILE], ALU.add),
           wait=[("V", C["V"])], inc=[("V", 1)])
        op("vector", lambda e: e.tensor_copy(xr_h[:], scr2[:]),
           wait=[("V", C["V"]), gate_xr], inc=[("V", 1)])
        op("vector", lambda e: e.tensor_tensor(xr_l[:], scr2[:], xr_h[:], ALU.subtract),
           wait=[("V", C["V"])], inc=[("V", 1)])
        op("vector", lambda e: e.tensor_tensor(scr2[:], scr1[:], muk_bc, ALU.mult),
           wait=[("V", C["V"])], inc=[("V", 1)])
        op("vector", lambda e: e.tensor_tensor(scr2[:], scr2[:], ln2[:, :, 0:TILE], ALU.add),
           wait=[("V", C["V"])], inc=[("V", 1)])
        op("vector", lambda e: e.tensor_copy(xk_h[:], scr2[:]),
           wait=[("V", C["V"]), gate_xk], inc=[("V", 1)])
        op("vector", lambda e: e.tensor_tensor(xk_l[:], scr2[:], xk_h[:], ALU.subtract),
           wait=[("V", C["V"])], inc=[("V", 1)])
        me["xk"] = ev("V")

    sview = scr2[:].rearrange("p a b -> p (a b)")[:, 0:D].unsqueeze(1)
    b_load_dma(0)
    ev_wo = load_split_weight(w_in["woT"], wB["woh"], wB["wol"], scale_nw=True,
                              first_wait=ev_A_drain, eng="vector", lsem="SW")
    op("vector", lambda e: e.memset(ln2[:], 0.0), inc=[("V", 1)], nop=True)
    ev_ln2init = ev("V")
    b_load_wo(0)
    ev_wr = load_split_weight(w_in["wrT"], wB["wrh"], wB["wrl"], first_wait=ev_A_drain,
                              eng="gpsimd", stage=sview, lsem="LB")
    ev_wv2 = load_split_weight(w_in["wv2T"], wB["wv2h"], wB["wv2l"], first_wait=[ev_wr],
                               eng="gpsimd", stage=sview, lsem="LB")
    state["scr2_ld"] = ev_wv2
    b_aobc(0)
    b_x2(0)
    b_ln2_xrxk(0)
    bld.flush()
    ev_wk2 = load_split_weight(w_in["wk2T"], wB["wk2h"], wB["wk2l"], first_wait=[ev_wo],
                               eng="vector", lsem="SW")
    bld.flush()
    for it in range(1, nt + 1):
        p, q = it, it - 1
        if p < nt:
            b_load_wo(p)
        if q >= 0:
            b_wr(q)
            b_wk2(q)
            b_wv2(q)
        if p < nt:
            b_aobc(p)
        if q >= 0:
            b_ffstat(q)
            b_fin(q)
        if p < nt:
            b_x2(p)
            b_ln2_xrxk(p)
        bld.flush()
    n_bstore = n_bstore[0]


    bld.wait("sync", ("SB", n_bstore * 16), ("SO0", C["SO0"]), ("SO1", C["SO1"]),
             ("SR", n_rstore * 16), ("SX", C["SX"]))
    bld.flush()
    ctxB.close()
    ctx.close()
    return nc


_WARM = 256
_SLICE = 4096


def _consts():
    import ml_dtypes
    tri = np.triu(np.ones((128, 128), dtype=np.float32))
    mask4 = np.tile(tri, (1, 4)).astype(ml_dtypes.bfloat16)
    return {
        "identv": np.eye(128, dtype=np.float32),
        "mask4v": mask4,
        "onescv": np.ones((128, 1), dtype=np.float32),
        "onesrv": np.ones((1, 128), dtype=np.float32),
    }


def _weight_maps(Wq, Wk, Wv, Wo, norm_w, W_r, W_k, W_v, mu_r, mu_k):
    m = {
        "wqT": np.ascontiguousarray(np.asarray(Wq, np.float32).T),
        "wkT": np.ascontiguousarray(np.asarray(Wk, np.float32).T),
        "wvT": np.ascontiguousarray(np.asarray(Wv, np.float32).T),
        "woT": np.ascontiguousarray(np.asarray(Wo, np.float32).T),
        "wrT": np.ascontiguousarray(np.asarray(W_r, np.float32).T),
        "wk2T": np.ascontiguousarray(np.asarray(W_k, np.float32).T),
        "wv2T": np.ascontiguousarray(np.asarray(W_v, np.float32).T),
        "nwv": np.ascontiguousarray(np.asarray(norm_w, np.float32).reshape(NCH, 128).T),
        "murv": np.ascontiguousarray(np.asarray(mu_r, np.float32).reshape(NCH, 128).T),
        "mukv": np.ascontiguousarray(np.asarray(mu_k, np.float32).reshape(NCH, 128).T),
    }
    m.update(_consts())
    return m


def _xt_block(sl):
    tok = sl.shape[0]
    return np.ascontiguousarray(
        sl.T.reshape(NCH, 128, tok // TILE, TILE).transpose(1, 2, 0, 3))


def kernel(L0_attn_x, L0_attn_Wq, L0_attn_Wk, L0_attn_Wv, L0_attn_Wo, L0_attn_norm_w,
           L0_ffn_W_r, L0_ffn_W_k, L0_ffn_W_v, L0_ffn_mu_r, L0_ffn_mu_k):
    x = np.asarray(L0_attn_x, dtype=np.float32)
    B, N, _ = x.shape
    tok = _WARM + _SLICE

    wmats = _weight_maps(L0_attn_Wq, L0_attn_Wk, L0_attn_Wv, L0_attn_Wo, L0_attn_norm_w,
                         L0_ffn_W_r, L0_ffn_W_k, L0_ffn_W_v, L0_ffn_mu_r, L0_ffn_mu_k)

    in_maps = []
    for core in range(8):
        b, j = core // 4, core % 4
        lo = j * _SLICE - _WARM
        if lo < 0:
            sl = np.concatenate([np.zeros((_WARM, D), np.float32),
                                 x[b, 0:(j + 1) * _SLICE]], axis=0)
        else:
            sl = x[b, lo:(j + 1) * _SLICE]
        m = dict(wmats)
        m["xT"] = _xt_block(sl)
        in_maps.append(m)

    nc = build_program(tok, _WARM)
    res = run_bass_kernel_spmd(nc, in_maps, core_ids=list(range(8)))
    out = np.empty((2, 16384, D), dtype=np.float32)
    for core in range(8):
        b, j = core // 4, core % 4
        out[b, j * _SLICE:(j + 1) * _SLICE] = res.results[core]["outT"].T
    return out


# version 31 (final)
# speedup vs baseline: 1.0634x; 1.0294x over previous
import numpy as np
import concourse.bass as bass
import concourse.mybir as mybir
from concourse.bass_utils import run_bass_kernel_spmd
from contextlib import ExitStack

F32 = mybir.dt.float32
BF16 = mybir.dt.bfloat16
AF = mybir.ActivationFunctionType
ALU = mybir.AluOpType

D = 1024
NCH = 8
H = 8
HD = 128
CH = 128
TILE = 256
MSH = 44.0
EPS_LN = 1e-5
EPS_RMS = 1e-6
INV_D = 1.0 / D
E_MSH = float(np.exp(-MSH))


class Bld:
    ENGS = ("sync", "vector", "scalar", "gpsimd", "tensor")

    def __init__(self, nc, ctx, sem_names):
        self.nc = nc
        self.block = ctx.enter_context(nc.Block())
        self.sem = {}
        self.cnt = {}
        for name in sem_names:
            self.sem[name] = ctx.enter_context(nc.semaphore(name))
            self.cnt[name] = 0
        self.q = {e: [] for e in self.ENGS}

    def op(self, eng, fn, wait=(), inc=(), nop=False):
        wmax = {}
        for s, v in wait:
            if v > 0:
                wmax[s] = max(v, wmax.get(s, 0))
        waits = tuple((self.sem[s], v) for s, v in wmax.items())
        ev = None
        incs = []
        for s, k in inc:
            self.cnt[s] += k
            incs.append((self.sem[s], k))
            ev = (s, self.cnt[s])

        def closure(e, fn=fn, waits=waits, incs=tuple(incs), nop=nop):
            for sem, v in waits:
                e.wait_ge(sem, v)
            if nop:
                e.engine_nop()
            ins = fn(e)
            for sem, k in incs:
                ins.then_inc(sem, k)

        self.q[eng].append(closure)
        return ev

    def wait(self, eng, *evs):
        wmax = {}
        for s, v in evs:
            if v > 0:
                wmax[s] = max(v, wmax.get(s, 0))
        waits = tuple((self.sem[s], v) for s, v in wmax.items())

        def closure(e, waits=waits):
            for sem, v in waits:
                e.wait_ge(sem, v)

        self.q[eng].append(closure)

    def flush(self):
        for e in self.ENGS:
            ops = self.q[e]
            if not ops:
                continue
            self.q[e] = []

            def body(engine, ops=ops):
                for f in ops:
                    f(engine)

            getattr(self.block, e)(body)

    def ev(self, s):
        return (s, self.cnt[s])


def build_program(tok, warm, debug=False):
    assert tok % TILE == 0 and warm % TILE == 0
    nt = tok // TILE
    out_tok = tok - warm

    nc = bass.Bass("TRN2", target_bir_lowering=False, dynamic_dma_scratch_size=2048)

    xT = nc.dram_tensor("xT", [128, tok // TILE, NCH, TILE], F32,
                        kind="ExternalInput").ap()
    w_in = {nm: nc.dram_tensor(nm, [D, D], F32, kind="ExternalInput").ap()
            for nm in ("wqT", "wkT", "wvT", "woT", "wrT", "wk2T", "wv2T")}
    nwv = nc.dram_tensor("nwv", [128, NCH], F32, kind="ExternalInput").ap()
    murv = nc.dram_tensor("murv", [128, NCH], F32, kind="ExternalInput").ap()
    mukv = nc.dram_tensor("mukv", [128, NCH], F32, kind="ExternalInput").ap()
    identv = nc.dram_tensor("identv", [128, 128], F32, kind="ExternalInput").ap()
    mask4v = nc.dram_tensor("mask4v", [128, 512], BF16, kind="ExternalInput").ap()
    onescv = nc.dram_tensor("onescv", [128, 1], F32, kind="ExternalInput").ap()
    onesrv = nc.dram_tensor("onesrv", [1, 128], F32, kind="ExternalInput").ap()

    outT = nc.dram_tensor("outT", [D, out_tok], F32, kind="ExternalOutput").ap()
    okind = dict(kind="ExternalOutput") if debug else {}
    o_scr = nc.dram_tensor("o_scr", [128, tok // TILE, NCH, TILE], F32, **okind).ap()
    rms_scr = nc.dram_tensor("rms_scr", [1, tok], F32, **okind).ap()
    x2_scr = nc.dram_tensor("x2_scr", [D, tok], F32, kind="ExternalOutput").ap() \
        if debug else None

    outTc = outT.rearrange("(c p) n -> p c n", p=128)

    ctx = ExitStack()
    bld = Bld(nc, ctx, ["LX", "LO", "LW", "LR", "SO0", "SO1", "SR", "SB", "SX",
                        "SW", "LB", "TT", "V", "S", "G", "T"])
    op, ev, C = bld.op, bld.ev, bld.cnt

    mask4 = ctx.enter_context(nc.sbuf_tensor("mask4", [128, 512], BF16))
    onesc = ctx.enter_context(nc.sbuf_tensor("onesc", [128, 1], F32))
    onescb = ctx.enter_context(nc.sbuf_tensor("onescb", [128, 1], BF16))
    onesr = ctx.enter_context(nc.sbuf_tensor("onesr", [1, 128], F32))
    nw = ctx.enter_context(nc.sbuf_tensor("nw", [128, NCH], F32))
    mur = ctx.enter_context(nc.sbuf_tensor("mur", [128, NCH], F32))
    muk = ctx.enter_context(nc.sbuf_tensor("muk", [128, NCH], F32))
    b_p44 = ctx.enter_context(nc.sbuf_tensor("b_p44", [128, 1], F32))
    b_m44 = ctx.enter_context(nc.sbuf_tensor("b_m44", [128, 1], F32))
    eps5 = ctx.enter_context(nc.sbuf_tensor("eps5", [1, 1], F32))
    eps6 = ctx.enter_context(nc.sbuf_tensor("eps6", [1, 1], F32))
    r_mean = ctx.enter_context(nc.sbuf_tensor("r_mean", [1, TILE], F32))
    r_msq = ctx.enter_context(nc.sbuf_tensor("r_msq", [1, TILE], F32))
    r_var = ctx.enter_context(nc.sbuf_tensor("r_var", [1, TILE], F32))
    r_a = ctx.enter_context(nc.sbuf_tensor("r_a", [1, TILE], F32))
    r_b = ctx.enter_context(nc.sbuf_tensor("r_b", [1, TILE], F32))
    r_rms = ctx.enter_context(nc.sbuf_tensor("r_rms", [1, TILE], F32))
    r_rr = r_rms
    r_t0 = r_msq
    wstage = ctx.enter_context(nc.sbuf_tensor("wstage", [128, 1, D], F32))

    p_pj = [ctx.enter_context(nc.psum_tensor(f"p_pj{i}", [128, 512], F32)) for i in range(2)]
    p_st = ctx.enter_context(nc.psum_tensor("p_st", [128, 512], F32))
    p_bc = ctx.enter_context(nc.psum_tensor("p_bc", [128, 512], F32))
    p_A = ctx.enter_context(nc.psum_tensor("p_A", [128, 512], F32))
    p_O = ctx.enter_context(nc.psum_tensor("p_O", [128, 512], F32))
    p_O2 = ctx.enter_context(nc.psum_tensor("p_O2", [128, 512], F32))
    p_A2 = ctx.enter_context(nc.psum_tensor("p_A2", [128, 512], F32))

    for dst, src in ((mask4, mask4v), (onesc, onescv), (onesr, onesrv),
                     (nw, nwv), (mur, murv), (muk, mukv)):
        op("sync", lambda e, dst=dst, src=src: e.dma_start(out=dst[:], in_=src),
           inc=[("LW", 16)])
    ev_cdma = ev("LW")
    op("vector", lambda e: e.memset(onescb[:], 1.0), inc=[("V", 1)], nop=True)
    op("vector", lambda e: e.memset(b_p44[:], MSH), inc=[("V", 1)], nop=True)
    op("vector", lambda e: e.memset(b_m44[:], -MSH), inc=[("V", 1)], nop=True)
    op("vector", lambda e: e.memset(eps5[:], EPS_LN), inc=[("V", 1)], nop=True)
    op("vector", lambda e: e.memset(eps6[:], EPS_RMS), inc=[("V", 1)], nop=True)

    def load_split_weight(src_ap, dst_h, dst_l, scale_nw=False, first_wait=(), eng="vector",
                          stage=None, lsem="LW"):
        E = "V" if eng == "vector" else "G"
        wst = wstage[:] if stage is None else stage
        srcc = src_ap.rearrange("(c p) n -> p c n", p=128)
        last = None
        for ci in range(NCH):
            wv = [(E, C[E])] + (list(first_wait) if ci == 0 else [])
            op("sync", lambda e, ci=ci, srcc=srcc, wst=wst: e.dma_start(
                out=wst, in_=srcc[:, ci:ci + 1, :]), wait=wv, inc=[(lsem, 16)])
            wl = [(lsem, C[lsem]), ev_cdma]
            if scale_nw:
                op(eng, lambda e, ci=ci, wst=wst: e.tensor_scalar_mul(
                    wst[:, 0, :], wst[:, 0, :], nw[:, ci:ci + 1]),
                   wait=wl + [(E, C[E])], inc=[(E, 1)])
                wl = [(E, C[E])]
            op(eng, lambda e, dst_h=dst_h, ci=ci, wst=wst: e.tensor_copy(
                dst_h[:, ci:ci + 1, :], wst), wait=wl + [(E, C[E])], inc=[(E, 1)])
            op(eng, lambda e, dst_h=dst_h, dst_l=dst_l, ci=ci, wst=wst: e.tensor_tensor(
                dst_l[:, ci:ci + 1, :], wst, dst_h[:, ci:ci + 1, :], ALU.subtract),
               wait=[(E, C[E])], inc=[(E, 1)])
            last = ev(E)
        return last

    state = {
        "bc_read": ("V", 0),
        "scr2_ld": ("G", 0),
        "bc_ev": ("T", 0),
        "pj_free": [("V", 0), ("V", 0)],
        "stats_free": ("V", 0),
    }

    def row_stats(eps_tile):
        op("vector", lambda e: e.tensor_scalar_mul(r_mean[:], p_st[0:1, 0:TILE], INV_D),
           wait=[("T", C["T"]), ("V", C["V"])], inc=[("V", 1)])
        op("vector", lambda e: e.tensor_tensor(r_msq[:], r_mean[:], r_mean[:], ALU.mult),
           wait=[("V", C["V"])], inc=[("V", 1)])
        op("vector", lambda e: e.scalar_tensor_tensor(
            r_var[:], p_st[32:33, 0:TILE], INV_D, r_msq[:], ALU.mult, ALU.subtract),
           wait=[("V", C["V"])], inc=[("V", 1)])
        op("scalar", lambda e: e.activation(out=r_t0[:], in_=r_var[:], func=AF.Sqrt,
                                            bias=eps_tile[:, 0:1]),
           wait=[("V", C["V"]), ("S", C["S"])], inc=[("S", 1)])
        op("vector", lambda e: e.reciprocal(r_a[:], r_t0[:]),
           wait=[("S", C["S"])], inc=[("V", 1)], nop=True)
        op("vector", lambda e: e.scalar_tensor_tensor(
            r_b[:], r_mean[:], -1.0, r_a[:], ALU.mult, ALU.mult),
           wait=[("V", C["V"])], inc=[("V", 1)])
        return ev("V")

    def bcast_rows():
        wv = [("V", C["V"]), state["bc_read"]]
        op("tensor", lambda e: e.matmul(p_bc[:, 0:TILE], onesr[:], r_a[:],
                                        start=True, stop=False), wait=wv)
        op("tensor", lambda e: e.matmul(p_bc[:, 256:256 + TILE], onesr[:], r_b[:],
                                        start=False, stop=True), inc=[("T", 1)])
        return ev("T")

    def bcA():
        return p_bc[:, 0:TILE].unsqueeze(1).broadcast_to([128, NCH, TILE])

    def bcB():
        return p_bc[:, 256:256 + TILE].unsqueeze(1).broadcast_to([128, NCH, TILE])

    def stats_reduce(x_ap, sq_ap, first_wait=()):
        wv = [state["stats_free"]] + list(first_wait)
        for ci in range(NCH):
            op("tensor", lambda e, ci=ci: e.matmul(
                p_st[0:1, 0:TILE], onesc[:], x_ap[:, ci, :],
                start=(ci == 0), stop=(ci == NCH - 1)),
               wait=wv if ci == 0 else ())
            op("tensor", lambda e, ci=ci: e.matmul(
                p_st[32:33, 0:TILE], onescb[:], sq_ap[:, ci, :],
                start=(ci == 0), stop=(ci == NCH - 1)),
               inc=[("T", 1)] if ci == NCH - 1 else ())
        return ev("T")

    def proj_feature(wh, wl, xh, xl, dj, extra_wait=()):
        bank = p_pj[dj % 2]
        wv = [state["pj_free"][dj % 2]] + list(extra_wait)
        first = True
        for ci in range(NCH):
            for (wop, xop) in ((wh, xh), (wh, xl), (wl, xh)):
                last = (ci == NCH - 1) and (wop is wl)
                op("tensor",
                   lambda e, wop=wop, xop=xop, ci=ci, dj=dj, first=first, last=last:
                   e.matmul(bank[:, 0:TILE], wop[:, ci, dj * 128:(dj + 1) * 128],
                            xop[:, ci, :], start=first, stop=last),
                   wait=wv if first else (), inc=[("T", 1)] if last else ())
                first = False
        return ev("T")

    ctxA = ExitStack()
    wA = {}
    for nm in ("wk", "wq", "wv"):
        wA[nm + "h"] = ctxA.enter_context(nc.sbuf_tensor(nm + "h", [128, NCH, D], BF16))
        wA[nm + "l"] = ctxA.enter_context(nc.sbuf_tensor(nm + "l", [128, NCH, D], BF16))
    x_sb = ctxA.enter_context(nc.sbuf_tensor("x_sb", [128, NCH, TILE], F32))
    qt_scr = ctxA.enter_context(nc.sbuf_tensor("qt_scr", [128, H, CH], F32))
    ln_h = ctxA.enter_context(nc.sbuf_tensor("ln_h", [128, NCH, TILE], BF16))
    ln_l = ctxA.enter_context(nc.sbuf_tensor("ln_l", [128, NCH, TILE], BF16))
    k_sb = ctxA.enter_context(nc.sbuf_tensor("k_sb", [128, H, TILE], F32))
    fq_sb = ctxA.enter_context(nc.sbuf_tensor("fq_sb", [128, 2, H, TILE], F32))
    g_sb = ctxA.enter_context(nc.sbuf_tensor("g_sb", [128, H, TILE], F32))
    xsq_bf = ctxA.enter_context(nc.sbuf_tensor("xsq_bf", [128, NCH, TILE], BF16))
    b_sb = ctxA.enter_context(nc.sbuf_tensor("b_sb", [128, H, CH], F32))
    ebq = ctxA.enter_context(nc.sbuf_tensor("ebq", [128, H, CH], F32))
    ebk = ctxA.enter_context(nc.sbuf_tensor("ebk", [128, H, CH], F32))
    qt_h = ctxA.enter_context(nc.sbuf_tensor("qt_h", [128, 2, H, CH], BF16))
    qt_l = ctxA.enter_context(nc.sbuf_tensor("qt_l", [128, 2, H, CH], BF16))
    kt_h = ctxA.enter_context(nc.sbuf_tensor("kt_h", [128, 2, H, CH], BF16))
    kt_l = ctxA.enter_context(nc.sbuf_tensor("kt_l", [128, 2, H, CH], BF16))
    A_h = ctxA.enter_context(nc.sbuf_tensor("A_h", [128, 512], BF16))
    A_l = ctxA.enter_context(nc.sbuf_tensor("A_l", [128, 512], BF16))
    A2_h = ctxA.enter_context(nc.sbuf_tensor("A2_h", [128, 512], BF16))
    A2_l = ctxA.enter_context(nc.sbuf_tensor("A2_l", [128, 512], BF16))
    osq_bf = xsq_bf[:][:, 0:4, 0:CH]
    o_stage = ctxA.enter_context(nc.sbuf_tensor("o_stage", [128, 4, CH], F32))
    vt_h = ctxA.enter_context(nc.sbuf_tensor("vt_h", [128, 2, H, HD], BF16))
    vt_l = ctxA.enter_context(nc.sbuf_tensor("vt_l", [128, 2, H, HD], BF16))
    kx_h = ctxA.enter_context(nc.sbuf_tensor("kx_h", [128, 3, H, CH], BF16))
    kx_l = ctxA.enter_context(nc.sbuf_tensor("kx_l", [128, 3, H, CH], BF16))
    Ax_h = ctxA.enter_context(nc.sbuf_tensor("Ax_h", [128, 2, 512], BF16))
    Ax_l = ctxA.enter_context(nc.sbuf_tensor("Ax_l", [128, 2, 512], BF16))

    ev_wk = load_split_weight(w_in["wkT"], wA["wkh"], wA["wkl"], eng="vector",
                              lsem="TT")
    ev_wq = load_split_weight(w_in["wqT"], wA["wqh"], wA["wql"], first_wait=[ev_wk],
                              eng="gpsimd", lsem="TT")
    ev_wv = load_split_weight(w_in["wvT"], wA["wvh"], wA["wvl"], first_wait=[ev_wq],
                              eng="vector", lsem="TT")
    bld.flush()

    st = {
        "x_free": ("V", 0),
        "fq_free": [("V", 0), ("V", 0)],
        "k_free": ("G", 0),
        "g_free": ("V", 0),
        "Abank_free": [("V", 0), ("V", 0)],
        "Obank_free": [[("SO0", 0), ("S", 0)], [("SO1", 0), ("S", 0)]],
        "ebq_free": ("V", 0),
        "ebx_done": ("S", 0),
        "ebk_free": ("G", 0),
        "osq_free": ("T", 0),
        "kt_free": [("T", 0), ("T", 0)],
        "vt_free": [("T", 0), ("T", 0)],
        "vt_ev": [("V", 0), ("V", 0)],
        "qt_free": [("T", 0), ("T", 0)],
        "kx_free": [("T", 0), ("T", 0), ("T", 0)],
        "kx_ready": [("G", 0), ("G", 0), ("G", 0)],
        "rms_free": [("V", 0), ("V", 0)],
        "stats_read": ("V", 0),
    }
    at = [dict(ev_g=("S", 0), ev_k=("G", 0), ev_q=("S", 0), ev_ln=("V", 0))
          for _ in range(nt + 1)]
    prep = {}
    n_ostore = 0
    n_rstore = [0]

    def a_prep(gc):
        q, cc = gc // 2, gc % 2
        tsl = slice(cc * CH, (cc + 1) * CH)
        par = gc % 2
        tpar = gc % 3
        fpar = q % 2
        aq = at[q]
        for h in range(H):
            wv = [aq["ev_g"], ("V", C["V"]), st["ebx_done"]] if h == 0 else ()
            op("vector", lambda e, h=h, tsl=tsl: e.tensor_tensor_scan(
                b_sb[:, h, :], g_sb[:, h, tsl], g_sb[:, h, tsl], 0.0, ALU.add, ALU.bypass),
               wait=wv, inc=[("V", 1)], nop=True)
        ev_b = ev("V")
        ev_ebq = op("scalar", lambda e: e.activation(out=ebq[:], in_=b_sb[:], func=AF.Exp,
                                                     bias=b_p44[:, 0:1]),
                    wait=[ev_b, st["ebq_free"], ("S", C["S"])], inc=[("S", 1)])
        ev_ebk = op("scalar", lambda e: e.activation(out=ebk[:], in_=b_sb[:], func=AF.Exp,
                                                     bias=b_m44[:, 0:1], scale=-1.0),
                    wait=[ev_b, st["ebk_free"]], inc=[("S", 1)])
        op("vector", lambda e, tsl=tsl, fpar=fpar: e.tensor_tensor(
            qt_scr[:], fq_sb[:, fpar, :, tsl], ebq[:], ALU.mult),
           wait=[aq["ev_q"], ev_ebq, ("V", C["V"])], inc=[("V", 1)])
        ev_qtmul = ev("V")
        op("vector", lambda e, par=par: e.tensor_copy(qt_h[:, par], qt_scr[:]),
           wait=[st["qt_free"][par], ("V", C["V"])], inc=[("V", 1)])
        op("vector", lambda e, par=par: e.tensor_tensor(qt_l[:, par], qt_scr[:],
                                                        qt_h[:, par], ALU.subtract),
           wait=[("V", C["V"])], inc=[("V", 1)])
        ev_qt = ev("V")
        op("gpsimd", lambda e, tsl=tsl: e.tensor_tensor(ebk[:], k_sb[:, :, tsl], ebk[:],
                                                        ALU.mult),
           wait=[aq["ev_k"], ev_ebk, ("G", C["G"])], inc=[("G", 1)])
        ev_ktmul = ev("G")
        ev_kth = op("scalar", lambda e, par=par: e.activation(
            out=kt_h[:, par], in_=ebk[:], func=AF.Copy),
            wait=[st["kt_free"][par], ev_ktmul, ("S", C["S"])], inc=[("S", 1)])
        op("gpsimd", lambda e, par=par: e.tensor_tensor(kt_l[:, par], ebk[:], kt_h[:, par],
                                                        ALU.subtract),
           wait=[ev_kth, ("G", C["G"])], inc=[("G", 1)])
        ev_kt = ev("G")
        for h in range(H):
            op("scalar", lambda e, h=h: e.activation(
                out=ebq[:, h, :], in_=b_sb[:, h, :], func=AF.Exp, scale=-1.0,
                bias=b_sb[:, h, CH - 1:CH]),
               wait=[ev_qtmul, ev_b, ("S", C["S"])], inc=[("S", 1)])
        ev_ebx = ev("S")
        st["ebx_done"] = ev_ebx
        ev_kxmul = op("gpsimd", lambda e, tsl=tsl: e.tensor_tensor(
            ebq[:], k_sb[:, :, tsl], ebq[:], ALU.mult),
           wait=[ev_ebx, ("G", C["G"])], inc=[("G", 1)])
        ev_kxh = op("scalar", lambda e, tpar=tpar: e.activation(
            out=kx_h[:, tpar], in_=ebq[:], func=AF.Copy),
            wait=[st["kx_free"][tpar], ev_kxmul, ("S", C["S"])], inc=[("S", 1)])
        op("gpsimd", lambda e, tpar=tpar: e.tensor_tensor(kx_l[:, tpar], ebq[:],
                                                          kx_h[:, tpar], ALU.subtract),
           wait=[ev_kxh, ("G", C["G"])], inc=[("G", 1)])
        ev_kx = ev("G")
        st["kx_ready"][tpar] = ev_kx
        st["ebq_free"] = ev_kx
        st["ebk_free"] = ev_kt
        st["k_free"] = ("G", C["G"])
        if cc == 1:
            st["fq_free"][fpar] = ev_qtmul
            st["g_free"] = ev_b
        prep.setdefault(gc, {}).update(
            dict(ev_b=ev_b, ev_qt=ev_qt, ev_kt=ev_kt, par=par, tpar=tpar))

    def a_vproj(gc):
        q, cc = gc // 2, gc % 2
        tsl = slice(cc * CH, (cc + 1) * CH)
        par = gc % 2
        for half in range(2):
            bank = p_pj[half]
            first = True
            for ci in range(NCH):
                for (wop, lnop) in ((wA["wvh"], ln_h), (wA["wvh"], ln_l), (wA["wvl"], ln_h)):
                    last = (ci == NCH - 1) and (wop is wA["wvl"])
                    op("tensor",
                       lambda e, wop=wop, lnop=lnop, ci=ci, half=half, first=first,
                       last=last, tsl=tsl, bank=bank:
                       e.matmul(bank[:], lnop[:, ci, tsl],
                                wop[:, ci, half * 512:(half + 1) * 512],
                                start=first, stop=last),
                       wait=[state["pj_free"][half], st["vt_free"][par], ev_wv]
                       if first else (),
                       inc=[("T", 1)] if last else ())
                    first = False
        ev_vmm = ev("T")
        op("vector", lambda e, par=par: e.tensor_copy(vt_h[:, par, 0:4, :], p_pj[0][:]),
           wait=[ev_vmm, ("V", C["V"])], inc=[("V", 1)])
        op("vector", lambda e, par=par: e.tensor_copy(vt_h[:, par, 4:8, :], p_pj[1][:]),
           wait=[("V", C["V"])], inc=[("V", 1)])
        op("vector", lambda e, par=par: e.tensor_tensor(
            vt_l[:, par, 0:4, :], p_pj[0][:], vt_h[:, par, 0:4, :], ALU.subtract),
           wait=[("V", C["V"])], inc=[("V", 1)])
        op("vector", lambda e, par=par: e.tensor_tensor(
            vt_l[:, par, 4:8, :], p_pj[1][:], vt_h[:, par, 4:8, :], ALU.subtract),
           wait=[("V", C["V"])], inc=[("V", 1)])
        ev_vev = ev("V")
        st["vt_ev"][par] = ev_vev
        state["pj_free"][0] = ev_vev
        state["pj_free"][1] = ev_vev
        prep.setdefault(gc, {})["ev_vev"] = ev_vev

    def a_gla(gc):
        q, cc = gc // 2, gc % 2
        tsl = slice(cc * CH, (cc + 1) * CH)
        p_ = prep[gc]
        par = p_["par"]
        xpar = (p_["tpar"] + 2) % 3
        first_chunk = (gc == 0)
        ev_vev = p_["ev_vev"]
        qh, ql = qt_h[:, par], qt_l[:, par]
        kh, kl = kt_h[:, par], kt_l[:, par]
        ev_Amm = [None, None]
        for g4 in range(2):
            hs = [4 * g4 + i for i in range(4)]
            bank = p_A if g4 == 0 else p_A2
            bfree = st["Abank_free"][g4]
            first = True
            for h in hs:
                for (kop, qop) in ((kh, qh), (kh, ql), (kl, qh)):
                    last = (h == hs[-1]) and (kop is kl)
                    op("tensor",
                       lambda e, kop=kop, qop=qop, h=h, first=first, last=last,
                       g4=g4, bank=bank:
                       e.matmul(bank[:, (h - 4 * g4) * 128:(h - 4 * g4 + 1) * 128],
                                kop[:, h, :], qop[:, h, :], start=first, stop=last),
                       wait=[p_["ev_kt"], p_["ev_qt"], bfree] if first else (),
                       inc=[("T", 1)] if last else ())
                    first = False
            ev_Amm[g4] = ev("T")
        ev_Aev = [None, None]
        for g4 in range(2):
            bank = p_A if g4 == 0 else p_A2
            Ah = A_h if g4 == 0 else A2_h
            Al = A_l if g4 == 0 else A2_l
            op("vector", lambda e, bank=bank: e.tensor_tensor(
                bank[:], bank[:], mask4[:], ALU.mult),
               wait=[ev_Amm[g4], ("V", C["V"])], inc=[("V", 1)])
            op("vector", lambda e, bank=bank, Ah=Ah: e.tensor_copy(Ah[:], bank[:]),
               wait=[("V", C["V"])], inc=[("V", 1)])
            op("vector", lambda e, bank=bank, Ah=Ah, Al=Al: e.tensor_tensor(
                Al[:], bank[:], Ah[:], ALU.subtract),
               wait=[("V", C["V"])], inc=[("V", 1)])
            ev_Aev[g4] = ev("V")
            st["Abank_free"][g4] = ev_Aev[g4]
        if not first_chunk:
            ev_Axmm = [None, None]
            for g4 in range(2):
                hs = [4 * g4 + i for i in range(4)]
                bank = p_A if g4 == 0 else p_A2
                first = True
                for h in hs:
                    for (kop, qop) in ((kx_h, qh), (kx_h, ql), (kx_l, qh)):
                        last = (h == hs[-1]) and (kop is kx_l)
                        op("tensor",
                           lambda e, kop=kop, qop=qop, h=h, first=first, last=last,
                           g4=g4, bank=bank, xpar=xpar:
                           e.matmul(bank[:, (h - 4 * g4) * 128:(h - 4 * g4 + 1) * 128],
                                    kop[:, xpar, h, :], qop[:, h, :],
                                    start=first, stop=last),
                           wait=[st["kx_ready"][xpar], p_["ev_qt"], ev_Aev[g4]]
                           if first else (),
                           inc=[("T", 1)] if last else ())
                        first = False
                ev_Axmm[g4] = ev("T")
            st["kx_free"][xpar] = ev_Axmm[1]
            ev_Axev = [None, None]
            for g4 in range(2):
                bank = p_A if g4 == 0 else p_A2
                op("vector", lambda e, bank=bank, g4=g4: e.tensor_scalar_mul(
                    Ax_h[:, g4, :], bank[:], E_MSH),
                   wait=[ev_Axmm[g4], ("V", C["V"])], inc=[("V", 1)])
                op("vector", lambda e, bank=bank, g4=g4: e.scalar_tensor_tensor(
                    Ax_l[:, g4, :], bank[:], E_MSH, Ax_h[:, g4, :], ALU.mult,
                    ALU.subtract),
                   wait=[("V", C["V"])], inc=[("V", 1)])
                ev_Axev[g4] = ev("V")
                st["Abank_free"][g4] = ev_Axev[g4]

        ev_rms_mm = None
        for g4 in range(2):
            hs = [4 * g4 + i for i in range(4)]
            Ah = A_h if g4 == 0 else A2_h
            Al = A_l if g4 == 0 else A2_l
            obank = p_O if g4 == 0 else p_O2
            first = True
            for hi_, h in enumerate(hs):
                for pi, (vop, aop) in enumerate(((vt_h, Ah), (vt_h, Al), (vt_l, Ah))):
                    last = first_chunk and (hi_ == 3) and (pi == 2)
                    op("tensor",
                       lambda e, vop=vop, aop=aop, h=h, g4=g4, first=first, last=last,
                       obank=obank, par=par:
                       e.matmul(obank[:, (h - 4 * g4) * 128:(h - 4 * g4 + 1) * 128],
                                vop[:, par, h, :],
                                aop[:, (h - 4 * g4) * 128:(h - 4 * g4 + 1) * 128],
                                start=first, stop=last),
                       wait=[ev_Aev[g4], ev_vev] + st["Obank_free"][g4]
                       if first else (),
                       inc=[("T", 1)] if last else ())
                    first = False
            if not first_chunk:
                ppar = 1 - par
                for hi_, h in enumerate(hs):
                    for pi, (vop, axop) in enumerate(((vt_h, Ax_h), (vt_h, Ax_l),
                                                      (vt_l, Ax_h))):
                        last = (hi_ == 3) and (pi == 2)
                        op("tensor",
                           lambda e, vop=vop, axop=axop, h=h, g4=g4, last=last,
                           obank=obank, ppar=ppar:
                           e.matmul(obank[:, (h - 4 * g4) * 128:(h - 4 * g4 + 1) * 128],
                                    vop[:, ppar, h, :],
                                    axop[:, g4, (h - 4 * g4) * 128:(h - 4 * g4 + 1) * 128],
                                    start=False, stop=last),
                           wait=[ev_Axev[g4], st["vt_ev"][ppar]]
                           if (hi_ == 0 and pi == 0) else (),
                           inc=[("T", 1)] if last else ())
            ev_Omm = ev("T")
            if g4 == 1:
                st["qt_free"][par] = ev_Omm
                st["kt_free"][par] = ev_Amm[1]
                if not first_chunk:
                    st["vt_free"][1 - par] = ev_Omm
                else:
                    st["vt_free"][1 - par] = ("T", C["T"])
            so = "SO0" if g4 == 0 else "SO1"
            ev_oev = op("scalar", lambda e, obank=obank: e.activation(
                out=o_stage[:], in_=obank[:].rearrange("p (a c) -> p a c", c=128),
                func=AF.Copy),
                wait=[ev_Omm, ("SO0", C["SO0"]), ("SO1", C["SO1"]), ("S", C["S"])],
                inc=[("S", 1)])
            ev_osq = op("scalar", lambda e: e.activation(
                out=osq_bf, in_=o_stage[:], func=AF.Square),
                wait=[ev_oev, st["osq_free"], ("S", C["S"])], inc=[("S", 1)])
            for i in range(4):
                op("tensor", lambda e, i=i, cc=cc, g4=g4: e.matmul(
                    p_st[64:65, 256 + cc * CH:256 + (cc + 1) * CH], onescb[:],
                    osq_bf[:, i, :], start=(g4 == 0 and i == 0),
                    stop=(g4 == 1 and i == 3)),
                   wait=[ev_osq, st["rms_free"][cc]] if i == 0 else (),
                   inc=[("T", 1)] if (g4 == 1 and i == 3) else ())
            st["osq_free"] = ("T", C["T"])
            ev_rms_mm = ("T", C["T"])
            op("sync", lambda e, g4=g4, q=q, cc=cc: e.dma_start(
                out=o_scr[:, q, 4 * g4:4 * g4 + 4, cc * CH:(cc + 1) * CH],
                in_=o_stage[:]),
               wait=[ev_oev], inc=[(so, 16)])
            st["Obank_free"][g4] = [("S", C["S"])]

        op("vector", lambda e, cc=cc: e.tensor_scalar_mul(
            r_rms[:, cc * CH:(cc + 1) * CH],
            p_st[64:65, 256 + cc * CH:256 + (cc + 1) * CH], INV_D),
           wait=[ev_rms_mm, ("SR", n_rstore[0] * 16), ("V", C["V"])], inc=[("V", 1)])
        op("scalar", lambda e, cc=cc: e.activation(
            out=r_rms[:, cc * CH:(cc + 1) * CH], in_=r_rms[:, cc * CH:(cc + 1) * CH],
            func=AF.Sqrt, bias=eps6[:, 0:1]),
           wait=[("V", C["V"]), ("S", C["S"])], inc=[("S", 1)])
        op("vector", lambda e, cc=cc: e.reciprocal(
            r_rms[:, cc * CH:(cc + 1) * CH], r_rms[:, cc * CH:(cc + 1) * CH]),
           wait=[("S", C["S"])], inc=[("V", 1)], nop=True)
        st["rms_free"][cc] = ev("V")
        if cc == 1:
            op("sync", lambda e, q=q: e.dma_start(
                out=rms_scr[0:1, q * TILE:q * TILE + TILE], in_=r_rms[:]),
               wait=[st["rms_free"][0], st["rms_free"][1]], inc=[("SR", 16)])
            n_rstore[0] += 1

    def a_stats(p):
        ts0 = p * TILE
        op("sync", lambda e, p=p: e.dma_start(out=x_sb[:], in_=xT[:, p]),
           wait=[st["x_free"]], inc=[("LX", 16)])
        ev_x = ev("LX")
        ev_sq = op("scalar", lambda e: e.activation(out=xsq_bf[:], in_=x_sb[:], func=AF.Square),
                   wait=[ev_x, ("T", C["T"]), state["stats_free"], ("S", C["S"])],
                   inc=[("S", 1)])
        stats_reduce(x_sb, xsq_bf[:], first_wait=[ev_x, ev_sq])
        row_stats(eps5)
        state["stats_free"] = ev("V")
        at[p]["ev_x"] = ev_x

    def a_bc(p):
        state["bc_ev"] = bcast_rows()

    def a_ln(p):
        op("vector", lambda e: e.tensor_tensor(x_sb[:], x_sb[:], bcA(), ALU.mult),
           wait=[state["bc_ev"], ("V", C["V"])], inc=[("V", 1)])
        op("vector", lambda e: e.tensor_tensor(x_sb[:], x_sb[:], bcB(), ALU.add),
           wait=[("V", C["V"])], inc=[("V", 1)])
        state["bc_read"] = ev("V")
        op("vector", lambda e: e.tensor_copy(ln_h[:], x_sb[:]),
           wait=[("V", C["V"])], inc=[("V", 1)])
        op("vector", lambda e: e.tensor_tensor(ln_l[:], x_sb[:], ln_h[:], ALU.subtract),
           wait=[("V", C["V"])], inc=[("V", 1)])
        ev_ln = ev("V")
        at[p]["ev_ln"] = ev_ln
        st["x_free"] = ev_ln

    def a_wk(p):
        fpar = p % 2
        ev_fev = None
        for dj in range(NCH):
            evt = proj_feature(wA["wkh"], wA["wkl"], ln_h, ln_l, dj,
                               extra_wait=[at[p]["ev_ln"], st["fq_free"][fpar], ev_wk]
                               if dj < 2 else [])
            ev_fev = op("scalar", lambda e, dj=dj, fpar=fpar: e.activation(
                out=fq_sb[:, fpar, dj, :], in_=p_pj[dj % 2][:, 0:TILE], func=AF.Sigmoid),
                wait=[evt], inc=[("S", 1)])
            state["pj_free"][dj % 2] = ev_fev
        wv = [ev_fev, st["g_free"], ("S", C["S"])]
        op("scalar", lambda e, fpar=fpar: e.activation(out=g_sb[:], in_=fq_sb[:, fpar],
                                                       func=AF.Sigmoid),
           wait=wv, inc=[("S", 1)])
        op("scalar", lambda e: e.activation(out=g_sb[:], in_=g_sb[:], func=AF.Ln),
           wait=[("S", C["S"])], inc=[("S", 1)])
        at[p]["ev_g"] = ev("S")
        at[p]["ev_k"] = op("gpsimd", lambda e, fpar=fpar: e.tensor_scalar(
            out=k_sb[:], in0=fq_sb[:, fpar], scalar1=-1.0, scalar2=1.0,
            op0=ALU.mult, op1=ALU.add),
            wait=[ev_fev, st["k_free"]], inc=[("G", 1)])

    def a_wq(p):
        fpar = p % 2
        ev_qev = None
        for dj in range(NCH):
            evt = proj_feature(wA["wqh"], wA["wql"], ln_h, ln_l, dj,
                               extra_wait=[ev_wq] if dj < 2 else [])
            ev_qev = op("scalar", lambda e, dj=dj, fpar=fpar: e.activation(
                out=fq_sb[:, fpar, dj, :], in_=p_pj[dj % 2][:, 0:TILE], func=AF.Silu),
                wait=[evt, at[p]["ev_g"], at[p]["ev_k"]] if dj == 0 else [evt],
                inc=[("S", 1)])
            state["pj_free"][dj % 2] = ev_qev
        at[p]["ev_q"] = ev_qev
        st["ln_free"] = ("T", C["T"])

    a_stats(0)
    for it in range(nt + 1):
        p, q = it, it - 1
        if p < nt:
            a_bc(p)
        if q >= 0:
            a_prep(2 * q)
            a_prep(2 * q + 1)
            a_vproj(2 * q)
            a_gla(2 * q)
            a_vproj(2 * q + 1)
        if p < nt:
            a_ln(p)
            a_wk(p)
        if q >= 0:
            a_gla(2 * q + 1)
        if p < nt:
            a_wq(p)
            if p + 1 < nt:
                a_stats(p + 1)
        bld.flush()

    n_rstore = n_rstore[0]
    ev_A_drain = [(s, C[s]) for s in ("V", "S", "G", "T")] + \
                 [("SO0", C["SO0"]), ("SO1", C["SO1"]), ("SR", n_rstore * 16),
                  ("LX", C["LX"])]
    bld.flush()
    ctxA.close()


    ctxB = ExitStack()
    wB = {}
    for nm in ("wo", "wr", "wk2", "wv2"):
        wB[nm + "h"] = ctxB.enter_context(nc.sbuf_tensor(nm + "h", [128, NCH, D], BF16))
        wB[nm + "l"] = ctxB.enter_context(nc.sbuf_tensor(nm + "l", [128, NCH, D], BF16))
    xB = ctxB.enter_context(nc.sbuf_tensor("xB", [128, NCH, TILE], F32))
    x2_sb = ctxB.enter_context(nc.sbuf_tensor("x2_sb", [128, NCH, TILE], F32))
    oB = ctxB.enter_context(nc.sbuf_tensor("oB", [128, NCH, TILE], F32))
    oB_bf = oB[:].bitcast(BF16)
    sqA = oB_bf[:, :, 0:TILE]
    sqB = oB_bf[:, :, TILE:2 * TILE]
    o_pair = ctxB.enter_context(nc.sbuf_tensor("o_pair", [128, NCH, 2, TILE], BF16))
    o_h = o_pair[:][:, :, 0, :]
    o_l = o_pair[:][:, :, 1, :]

    r_sb = ctxB.enter_context(nc.sbuf_tensor("r_sb", [128, NCH, TILE], F32))
    ffv = r_sb[:]
    ln2 = ctxB.enter_context(nc.sbuf_tensor("ln2", [128, NCH, TILE + 1], F32))
    scr1 = ctxB.enter_context(nc.sbuf_tensor("scr1", [128, NCH, TILE], F32))
    scr2 = ctxB.enter_context(nc.sbuf_tensor("scr2", [128, NCH, TILE], F32))
    scr2_bf = scr2[:].bitcast(BF16)[:, :, 0:TILE]
    xr_h = ctxB.enter_context(nc.sbuf_tensor("xr_h", [128, NCH, TILE], BF16))
    xr_l = ctxB.enter_context(nc.sbuf_tensor("xr_l", [128, NCH, TILE], BF16))
    xk_h = ctxB.enter_context(nc.sbuf_tensor("xk_h", [128, NCH, TILE], BF16))
    xk_l = ctxB.enter_context(nc.sbuf_tensor("xk_l", [128, NCH, TILE], BF16))
    kc_h = xk_h
    kc_l = xk_l


    TZ, VZ, SZ = ("T", 0), ("V", 0), ("S", 0)
    bt = [dict(xk=VZ, x2=VZ, x3=VZ, WoT=TZ, WrT=TZ, Wv2T=TZ, ff=VZ, kc=VZ, aoRM=VZ,
               ffstat=TZ, aostat=TZ, sqB_T=TZ, revac=SZ, ffsq=SZ) for _ in range(nt + 1)]
    state["pj_free"] = [("V", 0), ("V", 0)]
    n_bstore = [0]

    def b_wr(q):
        me = bt[q]
        ev_rev = None
        for dj in range(NCH):
            evt = proj_feature(wB["wrh"], wB["wrl"], xr_h, xr_l, dj,
                               extra_wait=[me["xk"], ev_wr] if dj < 2 else [])
            ev_rev = op("scalar", lambda e, dj=dj: e.activation(
                out=r_sb[:, dj, :], in_=p_pj[dj % 2][:, 0:TILE], func=AF.Sigmoid),
                wait=[evt, bt[q - 1]["x3"] if q > 0 else VZ,
                      bt[q - 1]["ffstat"] if q > 0 else TZ, ("S", C["S"])]
                if dj == 0 else [evt],
                inc=[("S", 1)])
            state["pj_free"][dj % 2] = ev_rev
        me["WrT"] = ("T", C["T"])
        me["revac"] = ev("S")
        me["r"] = ev_rev

    def b_load_dma(p):
        ts0 = p * TILE
        pv = bt[p - 1] if p > 0 else bt[nt]
        drain = list(ev_A_drain) if p == 0 else []
        op("gpsimd", lambda e, p=p: e.dma_start(out=oB[:], in_=o_scr[:, p]),
           wait=[("SO0", C["SO0"]), ("SO1", C["SO1"]), pv["sqB_T"], pv["aostat"]] + drain,
           inc=[("LO", 16)])
        op("gpsimd", lambda e, ts0=ts0: e.dma_start(out=r_rr[:], in_=rms_scr[0:1, ts0:ts0 + TILE]),
           wait=[("SR", n_rstore * 16), pv["aoRM"]] + drain, inc=[("LR", 16)])
        if p == 0:
            op("sync", lambda e: e.dma_start(out=xB[:], in_=xT[:, 0]),
               wait=drain, inc=[("LX", 16)])

    def b_load_wo(p):
        pv = bt[p - 1] if p > 0 else bt[nt]
        me = bt[p]
        if p > 0:
            b_load_dma(p)
        ev_o = ev("LO")
        ev_rrow = ev("LR")
        me["x_ld"] = ("LX", C["LX"])
        op("gpsimd", lambda e: e.tensor_copy(o_h, oB[:]),
           wait=[ev_o, pv["WoT"], pv["revac"], ("G", C["G"])], inc=[("G", 1)])
        op("gpsimd", lambda e: e.tensor_tensor(o_l, oB[:], o_h, ALU.subtract),
           wait=[("G", C["G"])], inc=[("G", 1)])
        ev_ospl = ev("G")
        ev_aoev = None
        for dj in range(NCH):
            evt = proj_feature(wB["woh"], wB["wol"], o_h, o_l, dj,
                               extra_wait=[ev_ospl, ev_wo] if dj < 2 else [])
            ev_aoev = op("scalar", lambda e, dj=dj: e.activation(
                out=scr1[:, dj, :], in_=p_pj[dj % 2][:, 0:TILE], func=AF.Copy),
                wait=[evt, pv["xk"], ("S", C["S"])], inc=[("S", 1)])
            state["pj_free"][dj % 2] = ev_aoev
        me["WoT"] = ("T", C["T"])
        ev_sq = op("scalar", lambda e: e.activation(out=sqA, in_=scr1[:], func=AF.Square),
                   wait=[ev_aoev, ("LO", C["LO"]), state["stats_free"], ("S", C["S"])],
                   inc=[("S", 1)])
        stats_reduce(scr1, sqA, first_wait=[ev_aoev, ev_sq])
        me["aostat"] = ("T", C["T"])
        op("vector", lambda e: e.tensor_scalar_mul(r_mean[:], p_st[0:1, 0:TILE], INV_D),
           wait=[("T", C["T"]), ("V", C["V"])], inc=[("V", 1)])
        op("vector", lambda e: e.tensor_tensor(r_msq[:], r_mean[:], r_mean[:], ALU.mult),
           wait=[("V", C["V"])], inc=[("V", 1)])
        op("vector", lambda e: e.scalar_tensor_tensor(
            r_var[:], p_st[32:33, 0:TILE], INV_D, r_msq[:], ALU.mult, ALU.subtract),
           wait=[("V", C["V"])], inc=[("V", 1)])
        op("vector", lambda e: e.tensor_tensor(r_t0[:], r_rr[:], r_rr[:], ALU.mult),
           wait=[ev_rrow, ("V", C["V"])], inc=[("V", 1)])
        op("vector", lambda e: e.tensor_tensor(r_var[:], r_var[:], r_t0[:], ALU.mult),
           wait=[("V", C["V"])], inc=[("V", 1)])
        op("scalar", lambda e: e.activation(out=r_t0[:], in_=r_var[:], func=AF.Sqrt,
                                            bias=eps5[:, 0:1]),
           wait=[("V", C["V"]), ("S", C["S"])], inc=[("S", 1)])
        op("vector", lambda e: e.reciprocal(r_var[:], r_t0[:]),
           wait=[("S", C["S"])], inc=[("V", 1)], nop=True)
        op("vector", lambda e: e.tensor_tensor(r_a[:], r_var[:], r_rr[:], ALU.mult),
           wait=[("V", C["V"])], inc=[("V", 1)])
        op("vector", lambda e: e.scalar_tensor_tensor(
            r_b[:], r_mean[:], -1.0, r_a[:], ALU.mult, ALU.mult),
           wait=[("V", C["V"])], inc=[("V", 1)])
        state["stats_free"] = ev("V")
        me["aoRM"] = ev("V")

    def b_wk2(q):
        me = bt[q]
        ev_kev = None
        for dj in range(NCH):
            evt = proj_feature(wB["wk2h"], wB["wk2l"], xk_h, xk_l, dj,
                               extra_wait=[ev_wk2, me["xk"]] if dj < 2 else [])
            ev_kev = op("scalar", lambda e, dj=dj: e.activation(
                out=scr2[:, dj, :], in_=p_pj[dj % 2][:, 0:TILE], func=AF.Relu),
                wait=[evt, bt[q - 1]["ffstat"] if q > 0 else state["scr2_ld"],
                      ("S", C["S"])]
                if dj == 0 else [evt],
                inc=[("S", 1)])
            state["pj_free"][dj % 2] = ev_kev
        ev_wk2T = ("T", C["T"])
        op("vector", lambda e: e.tensor_tensor(scr2[:], scr2[:], scr2[:], ALU.mult),
           wait=[ev_kev, ("V", C["V"])], inc=[("V", 1)])
        op("vector", lambda e: e.tensor_copy(kc_h[:], scr2[:]),
           wait=[("V", C["V"]), ev_wk2T], inc=[("V", 1)])
        op("vector", lambda e: e.tensor_tensor(kc_l[:], scr2[:], kc_h[:], ALU.subtract),
           wait=[("V", C["V"])], inc=[("V", 1)])
        me["kc"] = ev("V")

    def b_wv2(q):
        me = bt[q]
        ev_ffev = None
        for dj in range(NCH):
            evt = proj_feature(wB["wv2h"], wB["wv2l"], kc_h, kc_l, dj,
                               extra_wait=[me["kc"], ev_wv2] if dj < 2 else [])
            ev_ffev = op("vector", lambda e, dj=dj: e.tensor_tensor(
                ffv[:, dj, :], p_pj[dj % 2][:, 0:TILE], r_sb[:, dj, :], ALU.mult),
                wait=[evt, me["r"], bt[q - 1]["ffsq"] if q > 0 else SZ,
                      bt[q - 1]["ffstat"] if q > 0 else TZ, ("V", C["V"])],
                inc=[("V", 1)])
            state["pj_free"][dj % 2] = ev_ffev
        me["Wv2T"] = ("T", C["T"])
        me["ff"] = ev_ffev

    def b_aobc(p):
        me = bt[p]
        bcast_rows()
        me["aobc"] = ("T", C["T"])
        op("vector", lambda e: e.tensor_tensor(scr1[:], scr1[:], bcA(), ALU.mult),
           wait=[me["aobc"], ("V", C["V"])], inc=[("V", 1)])
        op("vector", lambda e: e.tensor_tensor(scr1[:], scr1[:], bcB(), ALU.add),
           wait=[("V", C["V"])], inc=[("V", 1)])
        state["bc_read"] = ev("V")

    def b_ffstat(q):
        me = bt[q]
        ev_sq = op("scalar", lambda e: e.activation(out=scr2_bf, in_=ffv, func=AF.Square),
                   wait=[me["ff"], me["kc"], state["stats_free"], ("S", C["S"])],
                   inc=[("S", 1)])
        me["ffsq"] = ev_sq
        stats_reduce(ffv, scr2_bf, first_wait=[me["ff"], ev_sq])
        me["ffstat"] = ("T", C["T"])
        row_stats(eps5)
        state["stats_free"] = ev("V")

    def b_fin(q):
        ts0 = q * TILE
        me = bt[q]
        bcast_rows()
        op("vector", lambda e: e.tensor_tensor(ffv, ffv, bcA(), ALU.mult),
           wait=[("T", C["T"]), ("V", C["V"])], inc=[("V", 1)])
        op("vector", lambda e: e.tensor_tensor(ffv, ffv, bcB(), ALU.add),
           wait=[("V", C["V"])], inc=[("V", 1)])
        state["bc_read"] = ev("V")
        op("vector", lambda e: e.tensor_tensor(x2_sb[:], x2_sb[:], ffv, ALU.add),
           wait=[("V", C["V"]), ("SX", C["SX"])], inc=[("V", 1)])
        me["x3"] = ev("V")
        if ts0 >= warm:
            op("sync", lambda e, ts0=ts0: e.dma_start(
                out=outTc[:, :, ts0 - warm:ts0 - warm + TILE], in_=x2_sb[:]),
               wait=[me["x3"]], inc=[("SB", 16)])
            n_bstore[0] += 1

    def b_x2(p):
        ts0 = p * TILE
        me = bt[p]
        op("vector", lambda e: e.tensor_tensor(x2_sb[:], xB[:], scr1[:], ALU.add),
           wait=[me["x_ld"], ("SB", n_bstore[0] * 16), ("V", C["V"])], inc=[("V", 1)])
        me["x2"] = ev("V")
        if debug:
            op("sync", lambda e, ts0=ts0: e.dma_start(
                out=x2_scr.rearrange("(c p) n -> p c n", p=128)[:, :, ts0:ts0 + TILE],
                in_=x2_sb[:]), wait=[me["x2"]], inc=[("SX", 16)])
        if p + 1 < nt:
            op("sync", lambda e, p=p: e.dma_start(out=xB[:], in_=xT[:, p + 1]),
               wait=[me["x2"]], inc=[("LX", 16)])

    def b_ln2_xrxk(p):
        me = bt[p]
        pv = bt[p - 1] if p > 0 else bt[nt]
        ev_sq = op("scalar", lambda e: e.activation(out=sqB, in_=x2_sb[:], func=AF.Square),
                   wait=[me["x2"], ("LO", C["LO"]), ("S", C["S"]), state["stats_free"]],
                   inc=[("S", 1)])
        stats_reduce(x2_sb, sqB, first_wait=[me["x2"], ev_sq])
        me["sqB_T"] = ("T", C["T"])
        row_stats(eps5)
        state["stats_free"] = ev("V")
        bcast_rows()
        op("vector", lambda e: e.tensor_copy(ln2[:, :, 0:1], ln2[:, :, TILE:TILE + 1]),
           wait=[("V", C["V"]), ev_ln2init], inc=[("V", 1)])
        op("vector", lambda e: e.tensor_tensor(ln2[:, :, 1:TILE + 1], x2_sb[:], bcA(),
                                               ALU.mult),
           wait=[("T", C["T"]), ("V", C["V"])], inc=[("V", 1)])
        op("vector", lambda e: e.tensor_tensor(ln2[:, :, 1:TILE + 1], ln2[:, :, 1:TILE + 1],
                                               bcB(), ALU.add),
           wait=[("V", C["V"])], inc=[("V", 1)])
        state["bc_read"] = ev("V")
        ev_ln2 = ev("V")
        op("vector", lambda e: e.tensor_tensor(scr1[:], ln2[:, :, 1:TILE + 1],
                                               ln2[:, :, 0:TILE], ALU.subtract),
           wait=[ev_ln2], inc=[("V", 1)])
        mur_bc = mur[:].unsqueeze(2).broadcast_to([128, NCH, TILE])
        muk_bc = muk[:].unsqueeze(2).broadcast_to([128, NCH, TILE])
        q = p - 1
        gate_scr2 = bt[q]["ffstat"] if q >= 0 else state["scr2_ld"]
        gate_xr = bt[q]["WrT"] if q >= 0 else TZ
        gate_xk = bt[q]["Wv2T"] if q >= 0 else TZ
        op("vector", lambda e: e.tensor_tensor(scr2[:], scr1[:], mur_bc, ALU.mult),
           wait=[("V", C["V"]), gate_scr2, ("S", C["S"])], inc=[("V", 1)])
        op("vector", lambda e: e.tensor_tensor(scr2[:], scr2[:], ln2[:, :, 0:TILE], ALU.add),
           wait=[("V", C["V"])], inc=[("V", 1)])
        op("vector", lambda e: e.tensor_copy(xr_h[:], scr2[:]),
           wait=[("V", C["V"]), gate_xr], inc=[("V", 1)])
        op("vector", lambda e: e.tensor_tensor(xr_l[:], scr2[:], xr_h[:], ALU.subtract),
           wait=[("V", C["V"])], inc=[("V", 1)])
        op("vector", lambda e: e.tensor_tensor(scr2[:], scr1[:], muk_bc, ALU.mult),
           wait=[("V", C["V"])], inc=[("V", 1)])
        op("vector", lambda e: e.tensor_tensor(scr2[:], scr2[:], ln2[:, :, 0:TILE], ALU.add),
           wait=[("V", C["V"])], inc=[("V", 1)])
        op("vector", lambda e: e.tensor_copy(xk_h[:], scr2[:]),
           wait=[("V", C["V"]), gate_xk], inc=[("V", 1)])
        op("vector", lambda e: e.tensor_tensor(xk_l[:], scr2[:], xk_h[:], ALU.subtract),
           wait=[("V", C["V"])], inc=[("V", 1)])
        me["xk"] = ev("V")

    sview = scr2[:].rearrange("p a b -> p (a b)")[:, 0:D].unsqueeze(1)
    b_load_dma(0)
    ev_wo = load_split_weight(w_in["woT"], wB["woh"], wB["wol"], scale_nw=True,
                              first_wait=ev_A_drain, eng="vector", lsem="SW")
    op("vector", lambda e: e.memset(ln2[:], 0.0), inc=[("V", 1)], nop=True)
    ev_ln2init = ev("V")
    b_load_wo(0)
    ev_wr = load_split_weight(w_in["wrT"], wB["wrh"], wB["wrl"], first_wait=ev_A_drain,
                              eng="gpsimd", stage=sview, lsem="LB")
    ev_wv2 = load_split_weight(w_in["wv2T"], wB["wv2h"], wB["wv2l"], first_wait=[ev_wr],
                               eng="gpsimd", stage=sview, lsem="LB")
    state["scr2_ld"] = ev_wv2
    b_aobc(0)
    b_x2(0)
    b_ln2_xrxk(0)
    bld.flush()
    ev_wk2 = load_split_weight(w_in["wk2T"], wB["wk2h"], wB["wk2l"], first_wait=[ev_wo],
                               eng="vector", lsem="SW")
    bld.flush()
    for it in range(1, nt + 1):
        p, q = it, it - 1
        if p < nt:
            b_load_wo(p)
        if q >= 0:
            b_wr(q)
            b_wk2(q)
            b_wv2(q)
        if p < nt:
            b_aobc(p)
        if q >= 0:
            b_ffstat(q)
            b_fin(q)
        if p < nt:
            b_x2(p)
            b_ln2_xrxk(p)
        bld.flush()
    n_bstore = n_bstore[0]


    bld.wait("sync", ("SB", n_bstore * 16), ("SO0", C["SO0"]), ("SO1", C["SO1"]),
             ("SR", n_rstore * 16), ("SX", C["SX"]))
    bld.flush()
    ctxB.close()
    ctx.close()
    return nc


_WARM = 256
_SLICE = 4096


def _consts():
    import ml_dtypes
    tri = np.triu(np.ones((128, 128), dtype=np.float32))
    mask4 = np.tile(tri, (1, 4)).astype(ml_dtypes.bfloat16)
    return {
        "identv": np.eye(128, dtype=np.float32),
        "mask4v": mask4,
        "onescv": np.ones((128, 1), dtype=np.float32),
        "onesrv": np.ones((1, 128), dtype=np.float32),
    }


def _weight_maps(Wq, Wk, Wv, Wo, norm_w, W_r, W_k, W_v, mu_r, mu_k):
    m = {
        "wqT": np.ascontiguousarray(np.asarray(Wq, np.float32).T),
        "wkT": np.ascontiguousarray(np.asarray(Wk, np.float32).T),
        "wvT": np.ascontiguousarray(np.asarray(Wv, np.float32).T),
        "woT": np.ascontiguousarray(np.asarray(Wo, np.float32).T),
        "wrT": np.ascontiguousarray(np.asarray(W_r, np.float32).T),
        "wk2T": np.ascontiguousarray(np.asarray(W_k, np.float32).T),
        "wv2T": np.ascontiguousarray(np.asarray(W_v, np.float32).T),
        "nwv": np.ascontiguousarray(np.asarray(norm_w, np.float32).reshape(NCH, 128).T),
        "murv": np.ascontiguousarray(np.asarray(mu_r, np.float32).reshape(NCH, 128).T),
        "mukv": np.ascontiguousarray(np.asarray(mu_k, np.float32).reshape(NCH, 128).T),
    }
    m.update(_consts())
    return m


def _xt_block(sl):
    tok = sl.shape[0]
    return np.ascontiguousarray(
        sl.T.reshape(NCH, 128, tok // TILE, TILE).transpose(1, 2, 0, 3))


def kernel(L0_attn_x, L0_attn_Wq, L0_attn_Wk, L0_attn_Wv, L0_attn_Wo, L0_attn_norm_w,
           L0_ffn_W_r, L0_ffn_W_k, L0_ffn_W_v, L0_ffn_mu_r, L0_ffn_mu_k):
    x = np.asarray(L0_attn_x, dtype=np.float32)
    B, N, _ = x.shape
    tok = _WARM + _SLICE

    wmats = _weight_maps(L0_attn_Wq, L0_attn_Wk, L0_attn_Wv, L0_attn_Wo, L0_attn_norm_w,
                         L0_ffn_W_r, L0_ffn_W_k, L0_ffn_W_v, L0_ffn_mu_r, L0_ffn_mu_k)

    in_maps = []
    for core in range(8):
        b, j = core // 4, core % 4
        lo = j * _SLICE - _WARM
        if lo < 0:
            sl = np.concatenate([np.zeros((_WARM, D), np.float32),
                                 x[b, 0:(j + 1) * _SLICE]], axis=0)
        else:
            sl = x[b, lo:(j + 1) * _SLICE]
        m = dict(wmats)
        m["xT"] = _xt_block(sl)
        in_maps.append(m)

    nc = build_program(tok, _WARM)
    res = run_bass_kernel_spmd(nc, in_maps, core_ids=list(range(8)))
    out = np.empty((2, 16384, D), dtype=np.float32)
    for core in range(8):
        b, j = core // 4, core % 4
        out[b, j * _SLICE:(j + 1) * _SLICE] = res.results[core]["outT"].T
    return out
